# version 05
# baseline (speedup 1.0000x reference)
import numpy as np
import concourse.bass as bass
import concourse.mybir as mybir
from concourse.bass_utils import run_bass_kernel_spmd
from contextlib import ExitStack

F32 = mybir.dt.float32
BF16 = mybir.dt.bfloat16
Alu = mybir.AluOpType
Act = mybir.ActivationFunctionType
AX = mybir.AxisListType.X

D = 1024
L = 2048
NSEQ = 2
TT = 16
DC = 8
II = 2816
ICN = 22
ICB = 11
SCALE = 0.03125
EPS = 1e-5


class Sem:
    def __init__(self, h):
        self.h = h
        self.n = 0


def build_nc():
    import os
    nc = bass.Bass("TRN2", target_bir_lowering=False,
                   debug=(os.environ.get("KDBG") == "1"),
                   detect_race_conditions=False)

    x_d = nc.dram_tensor("x", [NSEQ, L, D], F32, kind="ExternalInput").ap()
    wqkvT_d = nc.dram_tensor("wqkvT", [D, 3 * D], F32, kind="ExternalInput").ap()
    woT_d = nc.dram_tensor("woT", [D, D], F32, kind="ExternalInput").ap()
    bqkv_d = nc.dram_tensor("bqkv", [128, 16], F32, kind="ExternalInput").ap()
    bv_d = nc.dram_tensor("bv", [1, D], F32, kind="ExternalInput").ap()
    bo_d = nc.dram_tensor("bo", [1, D], F32, kind="ExternalInput").ap()
    wgs_d = nc.dram_tensor("wgs", [II, 128], F32, kind="ExternalInput").ap()
    wus_d = nc.dram_tensor("wus", [II, 128], F32, kind="ExternalInput").ap()
    wds_d = nc.dram_tensor("wds", [128, II], F32, kind="ExternalInput").ap()
    out_d = nc.dram_tensor("out", [NSEQ, L, D], F32, kind="ExternalOutput").ap()
    wq_st = nc.dram_tensor("wq_st", [3, 128, II], BF16, kind="Internal").ap()
    G_all = nc.dram_tensor("G_all", [8, 3, 128, II], BF16, kind="Internal",
                           addr_space="Shared").ap()

    ktap = "ExternalOutput" if os.environ.get("KTAP") == "1" else "Internal"
    wgT_q = nc.dram_tensor("wgT_q", [D, II], BF16, kind=ktap).ap()
    wuT_q = nc.dram_tensor("wuT_q", [D, II], BF16, kind=ktap).ap()
    wdT_q = nc.dram_tensor("wdT_q", [II, D], BF16, kind=ktap).ap()
    qS_d = nc.dram_tensor("qS_d", [DC, 128, L], BF16, kind=ktap).ap()
    kS_d = nc.dram_tensor("kS_d", [DC, 128, L], BF16, kind=ktap).ap()
    vS_d = nc.dram_tensor("vS_d", [TT, 128, 16 * 65], BF16, kind=ktap).ap()
    x1_d = nc.dram_tensor("x1_d", [TT, 128, D], F32, kind=ktap).ap()
    xr_d = nc.dram_tensor("xr_d", [TT, 128, D], F32, kind=ktap).ap()
    hb_d = nc.dram_tensor("hb_d", [2, 64, 512], BF16).ap()

    es = ExitStack()
    sbt = lambda name, shape, dt: es.enter_context(nc.sbuf_tensor(name, shape, dt))
    psA = es.enter_context(nc.psum_tensor("psA", [128, 2048], F32))
    ps2a = es.enter_context(nc.psum_tensor("ps2a", [128, 1024], F32))
    ps2b = es.enter_context(nc.psum_tensor("ps2b", [128, 1024], F32))

    slabA = sbt("slabA", [128, 24576], BF16)
    slabB = sbt("slabB", [128, 16384], BF16)
    qp = sbt("qp", [128, 4096], BF16)
    kp = sbt("kp", [128, 4096], BF16)
    vp = sbt("vp", [128, 4160], BF16)
    wgu = sbt("wgu", [128, 4096], BF16)
    wdb = sbt("wdb", [128, 2048], BF16)
    PTb = sbt("PTb", [128, 4096], BF16)
    qtile = sbt("qtile", [128, 2048], BF16)
    qkstage = sbt("qkstage", [128, 2048], BF16)
    hbnc = sbt("hbnc", [128, 1024], BF16)

    qsc = sbt("qsc", [128, 5120], F32)
    n2 = [qsc[:, 0:512], qsc[:, 512:1024]]
    a2 = [qsc[:, 1024:1536], qsc[:, 1536:2048]]
    sg2 = [qsc[:, 2048:2560], qsc[:, 2560:3072]]
    acA2 = [qsc[:, 3072:3584], qsc[:, 3584:4096]]
    acB = qsc[:, 4096:4608]
    ss_ = qsc[:, 4608:5120]
    amx3 = sbt("amx3", [128, 768], F32)

    def qslot_views(j):
        b = (j % 4) * 192
        return (amx3[:, b:b + 64], amx3[:, b + 64:b + 128],
                amx3[:, b + 128:b + 192])
    pool = sbt("pool", [128, 7168], F32)
    stat = sbt("stat", [128, 512], F32)
    st_sx = stat[:, 0:64]
    st_sq = stat[:, 64:128]
    st_mean = stat[:, 128:192]
    st_ex2 = stat[:, 192:256]
    st_m2 = stat[:, 256:320]
    st_var = stat[:, 320:384]
    st_va = stat[:, 384:448]
    st_rs = stat[:, 448:512]
    dmy = sbt("dmy", [128, 1024], F32)
    xsqv = dmy[:, 0:1024]
    strip = sbt("strip", [128, 1024], F32)
    rbr = sbt("rbr", [128, 1024], F32)
    hh = strip[:, 0:512]
    sgb = rbr[:, 0:512]
    ones66 = sbt("ones66", [128, 66], F32)
    boS = sbt("boS", [128, 1024], F32)
    bvS = sbt("bvS", [128, 1024], F32)
    bqkvS = sbt("bqkvS", [128, 16], F32)

    xb = [pool[:, 0:1024], pool[:, 1024:2048]]
    x1b = [pool[:, 2048:3072], pool[:, 3072:4096]]
    ob = [pool[:, 4096:5120], pool[:, 5120:6144]]
    xt = pool[:, 6144:7168]
    woTS = slabA[:, 0:8192]

    sP = Sem(es.enter_context(nc.semaphore("sP")))
    sA = Sem(es.enter_context(nc.semaphore("sA")))
    sV = Sem(es.enter_context(nc.semaphore("sV")))
    names = ["sC", "sWF0", "sWF1", "sWF2", "sWF3", "sWF4", "sGC",
             "sWST", "sX0", "sX1",
             "sOB0", "sOB1",
             "sQK", "sVS0", "sVS1", "sPQ0", "sPQ1", "sPK0", "sPK1",
             "sPV0", "sPV1", "sHB0", "sHB1", "sHBL0", "sHBL1",
             "sX10", "sX11", "sWGU", "sWD0", "sWD1",
             "sTQ0", "sTQ1", "sTH0", "sTH1", "sWKV", "sCC"]
    S = {nm: Sem(es.enter_context(nc.semaphore(nm))) for nm in names}
    sC = S["sC"]
    sWF = [S["sWF0"], S["sWF1"], S["sWF2"], S["sWF3"], S["sWF4"]]
    sGC = S["sGC"]
    sCC = S["sCC"]
    sWST = S["sWST"]
    sX = [S["sX0"], S["sX1"]]
    sOB = [S["sOB0"], S["sOB1"]]
    sQK = S["sQK"]
    sVS = [S["sVS0"], S["sVS1"]]
    sPQ = [S["sPQ0"], S["sPQ1"]]
    sPK = [S["sPK0"], S["sPK1"]]
    sPV = [S["sPV0"], S["sPV1"]]
    sHB = [S["sHB0"], S["sHB1"]]
    sHBL = [S["sHBL0"], S["sHBL1"]]
    sX1 = [S["sX10"], S["sX11"]]
    sWGU = S["sWGU"]
    sWD = [S["sWD0"], S["sWD1"]]
    sTQ = [S["sTQ0"], S["sTQ1"]]
    sTH = [S["sTH0"], S["sTH1"]]
    sWKV = S["sWKV"]

    ops = {"t": [], "v": [], "s": [], "y": [], "g": []}

    def op(eng, fn, wait=(), inc=()):
        wait = [(s, v) for (s, v) in wait if v > 0]
        for s, v in wait:
            assert v <= s.n, f"wait on future value {v} > {s.n}"
        for s, k in inc:
            s.n += k
        vals = [(s.h, v) for (s, v) in wait]
        incs = [(s.h, k) for (s, k) in inc]

        def run(e):
            for h, v in vals:
                e.wait_ge(h, v)
            inst = fn(e)
            for h, k in incs:
                inst.then_inc(h, k)
        ops[eng].append(run)

    def dma(eng, sem, out, in_, wait=(), transpose=False):
        if transpose:
            op(eng, lambda e: e.dma_start_transpose(out, in_), wait=wait,
               inc=[(sem, 16)])
        else:
            op(eng, lambda e: e.dma_start(out=out, in_=in_), wait=wait,
               inc=[(sem, 16)])

    dma("y", sC, bqkvS[:, :], bqkv_d[:, :])
    dma("y", sC, bvS[:, :], bv_d.broadcast_to([128, D]))
    dma("y", sC, boS[:, :], bo_d.broadcast_to([128, D]))
    const_c = sC.n
    op("v", lambda e: e.memset(ones66[:, :], 1.0))
    op("v", lambda e: e.memset(amx3[:, :], 1.0))
    op("v", lambda e: e.memset(stat[:, :], 1.0))
    op("v", lambda e: e.memset(dmy[:, :], 0.0), inc=[(sV, 1)])

    qrec = []
    qctr = [0]
    slot_guard = {0: [], 1: [], 2: [], 3: []}

    def q_emit_ts(r):
        a, sc, _ = r["sv"]
        op("v", lambda e, sc=sc, a=a: e.tensor_scalar(
            sc[:, :], a[:, :], 1.0 / 6.0, 1e-12, Alu.mult, Alu.max))

    def q_emit_recip(r):
        _, sc, rr = r["sv"]
        op("v", lambda e, rr=rr, sc=sc: e.reciprocal(rr[:, :], sc[:, :]))

    qA_hist = []
    qB_hist = []
    STW = [(0.25, 0.5), (0.75, 0.5), (1.25, 0.5), (1.75, 0.5),
           (2.5, 1.0), (3.5, 1.0), (5.0, 2.0)]

    def q_emit_applyA(r):
        w, nb = r["w"], r["w"] // 16
        _, _, rr = r["sv"]
        b2 = r["k"] % 2
        src3 = r["src"].rearrange("p (b k) -> p b k", k=16)
        rc3 = rr[:, 0:nb].unsqueeze(2).broadcast_to([128, nb, 16])
        nv = n2[b2][:, 0:w]
        n3 = nv.rearrange("p (b k) -> p b k", k=16)
        pm2 = qA_hist[-2] if len(qA_hist) >= 2 else None
        op("v", lambda e, n3=n3, src3=src3, rc3=rc3: e.tensor_tensor(
            n3, src3, rc3, Alu.mult),
            wait=([(sA, pm2["absready"])] if pm2 else []), inc=[(sV, 1)])
        op("s", lambda e, w=w, nv=nv, b2=b2: e.activation(
            sg2[b2][:, 0:w], nv, Act.Sign),
            wait=[(sV, sV.n)], inc=[(sA, 1)])
        r["sgnready"] = sA.n
        av = a2[b2][:, 0:w]
        op("s", lambda e, av=av, nv=nv: e.activation(av, nv, Act.Abs),
           wait=([(sGC, pm2["poolq"])] if pm2 else []), inc=[(sA, 1)])
        r["absready"] = sA.n
        acc = acA2[b2][:, 0:w]
        vprev = [(sV, qB_hist[-1])] if qB_hist else []
        t0, w0 = STW[0]
        op("g", lambda e, acc=acc, av=av: e.tensor_scalar(
            acc, av, t0, w0, Alu.is_gt, Alu.mult),
            wait=[(sA, r["absready"])] + vprev)
        for t, wt in STW[1:]:
            op("g", lambda e, av=av, t=t, wt=wt, w=w: e.tensor_scalar(
                acB[:, 0:w], av, t, wt, Alu.is_gt, Alu.mult))
            op("g", lambda e, acc=acc, w=w: e.tensor_tensor(
                acc, acc, acB[:, 0:w], Alu.add))
        op("g", lambda e, acc=acc, w=w: e.tensor_copy(
            acB[:, 0:w], acc), inc=[(sGC, 1)])
        r["poolq"] = sGC.n
        qA_hist.append(r)

    def q_emit_applyB(r):
        w, nb = r["w"], r["w"] // 16
        _, sc, _ = r["sv"]
        b2 = r["k"] % 2
        sc3 = sc[:, 0:nb].unsqueeze(2).broadcast_to([128, nb, 16])
        s3 = ss_[:, 0:w].rearrange("p (b k) -> p b k", k=16)
        sg3 = sg2[b2][:, 0:w].rearrange("p (b k) -> p b k", k=16)
        op("v", lambda e, s3=s3, sg3=sg3, sc3=sc3: e.tensor_tensor(
            s3, sg3, sc3, Alu.mult), wait=[(sA, r["sgnready"])])
        op("v", lambda e, dst=r["dst"], w=w, b2=b2: e.tensor_tensor(
            dst[:, 0:w], acA2[b2][:, 0:w], ss_[:, 0:w], Alu.mult),
            wait=[(sGC, r["poolq"])] + list(slot_guard[r["slot"]]),
            inc=[(sV, 1)])
        vready = sV.n
        qB_hist.append(vready)
        flush_tr([(sV, vready)])
        tr = r["tr"]
        for j, (sem, dsl, extra) in enumerate(tr):
            pend_tr.append((sem, dsl, r["dst"][:, j * 128:(j + 1) * 128],
                            extra))
        if tr:
            slot_guard[r["slot"]] = [(tr[0][0], tr[0][0].n + len(tr) * 16)]
        if r["post"] is not None:
            r["post"](vready)

    def q_push(src, w, tr=(), wait_red=(), post=None):
        slot = qctr[0] % 4
        k = qctr[0]
        qctr[0] += 1
        sv = qslot_views(slot)
        nb = w // 16
        src3 = src.rearrange("p (b k) -> p b k", k=16)
        a = sv[0]
        op("v", lambda e, a=a, src3=src3, nb=nb: e.tensor_reduce(
            a[:, 0:nb], src3, AX, Alu.max, apply_absolute_value=True),
            wait=wait_red)
        qrec.append({"k": k, "w": w, "sv": sv, "slot": slot, "src": src,
                     "dst": qtile[:, slot * 512: slot * 512 + 512],
                     "tr": list(tr), "post": post})
        if len(qrec) >= 2:
            q_emit_ts(qrec[-2])
        if len(qrec) >= 3:
            q_emit_recip(qrec[-3])
        if len(qrec) >= 4:
            q_emit_applyA(qrec[-4])
        if len(qrec) >= 5:
            q_emit_applyB(qrec.pop(0))

    def q_flush():
        assert len(qrec) == 4, f"q_flush with {len(qrec)} pending"
        for _ in range(4):
            q_push(dmy[:, 0:256], 256)
        d1, d2, d3, d4 = qrec
        q_emit_ts(d4)
        q_emit_recip(d3)
        q_emit_applyA(d2)
        q_emit_applyB(d1)
        q_emit_recip(d4)
        q_emit_applyA(d3)
        q_emit_applyB(d2)
        q_emit_applyA(d4)
        q_emit_applyB(d3)
        q_emit_applyB(d4)
        qrec.clear()

    def ln_chain(srcb, dst, wait_first, wait_norm):
        src8 = srcb.rearrange("p (c w) -> p c w", w=128)
        op("s", lambda e: e.activation(xsqv, srcb, Act.Square),
           wait=list(wait_first) + [(sV, sV.n)], inc=[(sA, 1)])
        sqd = sA.n
        op("v", lambda e: e.tensor_reduce(st_sx[:, 0:1], srcb, AX, Alu.add),
           wait=wait_first)
        op("v", lambda e: e.tensor_reduce(st_sq[:, 0:1], xsqv, AX, Alu.add),
           wait=[(sA, sqd)])
        for k in (1, 2, 4):
            op("v", lambda e, k=k: e.tensor_copy(st_sx[:, k:2 * k], st_sx[:, 0:k]))
            op("v", lambda e, k=k: e.tensor_copy(st_sq[:, k:2 * k], st_sq[:, 0:k]))
        op("v", lambda e: e.tensor_scalar(st_mean, st_sx, 1.0 / D, None, Alu.mult))
        op("v", lambda e: e.tensor_scalar(st_ex2, st_sq, 1.0 / D, None, Alu.mult))
        op("v", lambda e: e.tensor_tensor(st_m2, st_mean, st_mean, Alu.mult))
        op("v", lambda e: e.tensor_tensor(st_var, st_ex2, st_m2, Alu.subtract))
        op("v", lambda e: e.tensor_scalar(st_va, st_var, EPS, None, Alu.add),
           inc=[(sV, 1)])
        vv = sV.n
        op("s", lambda e: e.activation(st_rs, st_va, Act.Ln), wait=[(sV, vv)])
        op("s", lambda e: e.activation(st_rs, st_rs, Act.Exp, scale=-0.5),
           inc=[(sA, 1)])
        aa = sA.n
        op("v", lambda e: e.engine_nop(), wait=[(sA, aa)] + wait_norm)
        mb = st_mean[:, 0:8].unsqueeze(2).broadcast_to([128, 8, 128])
        rb = st_rs[:, 0:8].unsqueeze(2).broadcast_to([128, 8, 128])
        xc8 = xsqv.rearrange("p (c w) -> p c w", w=128)
        op("v", lambda e: e.tensor_tensor(xc8, src8, mb, Alu.subtract))
        op("v", lambda e: e.tensor_tensor(
            dst.rearrange("p (c w) -> p c w", w=128), xc8, rb, Alu.mult),
            inc=[(sV, 1)])
        return sV.n

    pend_tr = []

    def flush_tr(wait_v):
        for i, (sem, dst, srcap, extra) in enumerate(pend_tr):
            dma("y", sem, dst, srcap,
                wait=(list(wait_v) + extra) if i == 0 else [], transpose=True)
        pend_tr.clear()

    def drain_dummy():
        op("v", lambda e: e.tensor_copy(dmy[:, 0:512], dmy[:, 512:1024]),
           inc=[(sV, 1)])
        return sV.n

    wq_buf_free = [0, 0, 0, 0, 0]
    wfs = [pool[:, 0:128], pool[:, 128:256], pool[:, 256:384],
           pool[:, 384:512], pool[:, 512:640]]
    dma("y", sWF[0], pool[:, 3072:5888], wds_d[:, :])
    wd_dv = sWF[0].n
    wti = 0
    for mi, (src_d, strip0) in enumerate([(wgs_d, 0), (wus_d, 2816)]):
        for it in range(ICN):
            sl = wti % 5
            wti += 1
            dma("y", sWF[sl], wfs[sl][:, :], src_d[it * 128:(it + 1) * 128, :],
                wait=[(sV, wq_buf_free[sl])])
            dv = sWF[sl].n
            tr = [(sTQ[qctr[0] % 2],
                   slabA[:, strip0 + it * 128: strip0 + it * 128 + 128], [])]

            def post(vready, sl=sl):
                wq_buf_free[sl] = sV.n
            q_push(wfs[sl][:, :], 128, tr=tr, wait_red=[(sWF[sl], dv)],
                   post=post)
    for ck in range(6):
        c0 = ck * 512
        w = min(512, II - c0)
        tr = []
        for j in range(w // 128):
            c = (c0 + j * 128) // 128
            tr.append((sTQ[qctr[0] % 2],
                       slabA[:, 5632 + c * 128: 5632 + c * 128 + 128], []))
        q_push(pool[:, 3072 + c0:3072 + c0 + w], w, tr=tr,
               wait_red=[(sWF[0], wd_dv)])
    q_flush()
    tqw = [(sTQ[0], sTQ[0].n), (sTQ[1], sTQ[1].n)]
    dma("y", sWST, wq_st[0], slabA[:, 0:2816], wait=tqw)
    dma("y", sWST, wq_st[1], slabA[:, 2816:5632])
    dma("y", sWST, wq_st[2], slabA[:, 5632:8448])
    w_done = sWST.n
    w_quant_v = sV.n

    def emit_collective():
        op("g", lambda e: e.collective_compute(
            "AllGather", Alu.bypass, [list(range(8))],
            ins=[wq_st[:, :, :]],
            outs=[G_all[:, :, :, :]]),
            wait=[(sWST, w_done)], inc=[(sCC, 1)])
        cc = sCC.n
        if os.environ.get("KTAP") == "1":
            dma("y", sC, wgT_q.rearrange("(r p) w -> r p w", r=8),
                G_all[:, 0, :, :], wait=[(sCC, cc)])
            dma("y", sC, wuT_q.rearrange("(r p) w -> r p w", r=8),
                G_all[:, 1, :, :])
            for rr in range(8):
                dma("y", sC,
                    wdT_q[:, rr * 128:(rr + 1) * 128]
                    .rearrange("(c q) d -> c q d", q=128),
                    G_all[rr, 2, :, :].rearrange("q (c d) -> c q d", d=128))
        return cc

    xb_free = [0, 0]
    x1b_free = [0, 0]
    ob_free = [0, 0]
    qk_free = 0
    hb_free = [0, 0]
    wgu_free = 0
    wdb_free = [0, 0]
    qp_free = [0, 0]
    kp_free = [0, 0]
    vp_free = [0, 0]
    PT_free = [0, 0]
    vst_free = [0, 0]
    xrst = [None] * TT
    x1st = [None] * TT

    def load_wqkv(wait0):
        for dc in range(DC):
            dma("g", sWKV, slabA[:, dc * 3072:(dc + 1) * 3072],
                wqkvT_d[dc * 128:(dc + 1) * 128, :],
                wait=(wait0 if dc == 0 else []))
        return sWKV.n

    wkv_ready = load_wqkv([(sWST, w_done)])
    cc_done = emit_collective()
    exp_done = 0
    down_evac_v = 0

    for s in range(NSEQ):
        seq_start_p = sP.n

        for tt in range(TT):
            sl = tt % 2
            dma("y", sX[sl], xb[sl][:, :], x_d[s, tt * 128:(tt + 1) * 128, :],
                wait=[(sV, xb_free[sl])] +
                     ([(sV, w_quant_v), (sWST, w_done)]
                      if (s == 0 and tt == 0) else []))
            dx = sX[sl].n
            qsl = tt % 2
            qt_ = qtile[:, qsl * 1024:(qsl + 1) * 1024]
            vq = ln_chain(xb[sl][:, :], qt_,
                          wait_first=[(sX[sl], dx)],
                          wait_norm=slot_guard[2 * qsl] +
                                    slot_guard[2 * qsl + 1])
            op("v", lambda e, o=ob[sl], b=xb[sl], q=qt_: e.tensor_tensor(
                o, b[:, :], q, Alu.add),
                wait=[(sOB[sl], ob_free[sl])], inc=[(sV, 1)])
            xb_free[sl] = sV.n
            dma("y", sOB[sl], xr_d[tt], ob[sl], wait=[(sV, sV.n)])
            ob_free[sl] = sOB[sl].n
            xrst[tt] = (sl, sOB[sl].n)
            flush_tr([(sV, vq)])
            for c in range(DC):
                extra = [(sP, seq_start_p)] if (tt == 0 and c == 0) else []
                pend_tr.append((sTQ[qsl],
                                slabB[:, c * 2048 + tt * 128:
                                      c * 2048 + tt * 128 + 128],
                                qt_[:, c * 128:(c + 1) * 128], extra))
            slot_guard[2 * qsl] = [(sTQ[qsl], sTQ[qsl].n + 8 * 16)]
            slot_guard[2 * qsl + 1] = [(sTQ[qsl], sTQ[qsl].n + 8 * 16)]
        flush_tr([(sV, drain_dummy())])
        xnT0, xnT1 = sTQ[0].n, sTQ[1].n

        for ec in range(2 * DC):
            stg = qkstage[:, 0:2048]
            for dc in range(DC):
                for qt in range(4):
                    w0 = []
                    if dc == 0 and qt == 0:
                        w0 = [(sA, sA.n), (sV, sV.n)]
                        if ec == 0:
                            w0 += [(sWKV, wkv_ready), (sTQ[0], xnT0),
                                   (sTQ[1], xnT1)]
                    op("t", lambda e, ec=ec, dc=dc, qt=qt: e.matmul(
                        psA[:, qt * 512:(qt + 1) * 512],
                        slabA[:, dc * 3072 + ec * 128: dc * 3072 + ec * 128 + 128],
                        slabB[:, dc * 2048 + qt * 512: dc * 2048 + qt * 512 + 512],
                        start=(dc == 0), stop=(dc == DC - 1)),
                        wait=w0,
                        inc=([(sP, 1)] if (dc == DC - 1 and qt == 3) else []))
            mmv = sP.n
            op("s", lambda e, ec=ec, t=stg: e.activation(
                t, psA[:, :], Act.Identity, bias=bqkvS[:, ec:ec + 1]),
                wait=[(sP, mmv), (sQK, qk_free)] +
                     ([(sC, const_c)] if (s == 0 and ec == 0) else []),
                inc=[(sA, 1)])
            ev = sA.n
            dst = (qS_d if ec < DC else kS_d)[ec % DC]
            dma("y", sQK, dst[:, :], stg, wait=[(sA, ev)])
            qk_free = sQK.n

        for tt in range(TT):
            pse = [ps2a, ps2b][tt % 2]
            for dc in range(DC):
                for nh2 in range(2):
                    w0 = []
                    if dc == 0 and nh2 == 0:
                        w0 = [(sV, sV.n)]
                    op("t", lambda e, tt=tt, dc=dc, nh2=nh2, p=pse: e.matmul(
                        p[:, nh2 * 512:(nh2 + 1) * 512],
                        slabB[:, dc * 2048 + tt * 128: dc * 2048 + tt * 128 + 128],
                        slabA[:, dc * 3072 + 2048 + nh2 * 512:
                              dc * 3072 + 2048 + nh2 * 512 + 512],
                        start=(dc == 0), stop=(dc == DC - 1)),
                        wait=w0,
                        inc=([(sP, 1)] if (dc == DC - 1 and nh2 == 1) else []))
            mmv = sP.n
            vsl = tt % 2
            vstage = PTb[:, vsl * 2048:vsl * 2048 + 1040]
            v3 = vstage.rearrange("p (h c) -> p h c", c=65)
            op("v", lambda e, v3=v3: e.memset(v3[:, :, 64:65], 1.0),
               wait=[(sP, mmv), (sVS[vsl], vst_free[vsl])] +
                    ([(sC, const_c)] if (s == 0 and tt == 0) else []))
            op("v", lambda e, v3=v3, p=pse: e.tensor_tensor(
                v3[:, :, 0:64], p[:, :].rearrange("p (h c) -> p h c", c=64),
                bvS[:, :].rearrange("p (h c) -> p h c", c=64), Alu.add),
                inc=[(sV, 1)])
            dma("y", sVS[vsl], vS_d[tt], vstage, wait=[(sV, sV.n)])
            vst_free[vsl] = sVS[vsl].n
        stage_done = [(sQK, sQK.n), (sVS[0], sVS[0].n), (sVS[1], sVS[1].n)]
        qkv_done_p = sP.n
        qkv_done_v = sV.n

        av_guard_v = qkv_done_v
        rep_guard_v = qkv_done_v
        rep_guard_p = sP.n
        if exp_done == 0:
            exp_done = sA.n
        hb_slot = 0
        first_exp = True
        for p in range(DC):
            slq = p % 2
            dma("y", sPQ[slq], qp[:, slq * 2048:(slq + 1) * 2048], qS_d[p],
                wait=[(sP, qp_free[slq])] + (stage_done if p < 2 else []))
            dma("y", sPK[slq], kp[:, slq * 2048:(slq + 1) * 2048], kS_d[p],
                wait=[(sP, kp_free[slq])])
            for g4 in range(4):
                dma("y", sPV[slq],
                    vp[:, slq * 2080 + g4 * 520: slq * 2080 + (g4 + 1) * 520]
                    .rearrange("p (t c) -> p t c", c=130),
                    vS_d[g4 * 4:(g4 + 1) * 4, :, p * 130:(p + 1) * 130]
                    .transpose([1, 0, 2]),
                    wait=[(sP, vp_free[slq])] if g4 == 0 else [])
            pair_ld = [(sPQ[slq], sPQ[slq].n), (sPK[slq], sPK[slq].n),
                       (sPV[slq], sPV[slq].n)]
            qpb = qp[:, slq * 2048:(slq + 1) * 2048]
            kpb = kp[:, slq * 2048:(slq + 1) * 2048]
            vpb = vp[:, slq * 2080:(slq + 1) * 2080].rearrange(
                "p (t c) -> p t c", c=130)
            for qc in range(4):
                for ktp in range(8):
                    ptsl = ktp % 2
                    PT = PTb[:, ptsl * 2048:(ptsl + 1) * 2048]
                    for kt2 in range(2):
                        kt = ktp * 2 + kt2
                        w0 = []
                        if kt2 == 0:
                            w0 = [(sA, exp_done)]
                            if qc == 0 and ktp == 0:
                                w0 += pair_ld
                        op("t", lambda e, kt=kt, kt2=kt2, q=qpb, k=kpb, qc=qc:
                            e.matmul(
                                psA[:, kt2 * 512:(kt2 + 1) * 512],
                                k[0:64, kt * 128:(kt + 1) * 128],
                                q[0:64, qc * 512:(qc + 1) * 512],
                                start=True, stop=True, tile_position=(0, 0)),
                            wait=w0)
                        op("t", lambda e, kt=kt, kt2=kt2, q=qpb, k=kpb, qc=qc:
                            e.matmul(
                                psA[:, 1024 + kt2 * 512: 1024 + (kt2 + 1) * 512],
                                k[64:128, kt * 128:(kt + 1) * 128],
                                q[64:128, qc * 512:(qc + 1) * 512],
                                start=True, stop=True, tile_position=(64, 0)),
                            inc=([(sP, 1)] if kt2 == 1 else []))
                    sc_done = sP.n
                    ew = [(sP, sc_done), (sP, PT_free[ptsl])]
                    if first_exp:
                        ew += stage_done
                        if ktp == 1:
                            first_exp = False
                    op("s", lambda e, PT=PT: e.activation(PT, psA[:, :], Act.Exp,
                                                          scale=SCALE),
                       wait=ew, inc=[(sA, 1)])
                    exp_done = sA.n
                    for kt2 in range(2):
                        kt = ktp * 2 + kt2
                        w0 = []
                        if kt2 == 0:
                            w0 = [(sA, exp_done)]
                            if ktp == 0:
                                w0 += [(sV, av_guard_v)]
                        op("t", lambda e, kt=kt, kt2=kt2, ktp=ktp, v=vpb, PT=PT:
                            e.matmul(
                                ps2a[0:65, 0:512], v[:, kt, 0:65],
                                PT[:, kt2 * 512:(kt2 + 1) * 512],
                                start=(ktp == 0 and kt2 == 0),
                                stop=(ktp == 7 and kt2 == 1)),
                            wait=w0)
                        op("t", lambda e, kt=kt, kt2=kt2, ktp=ktp, v=vpb, PT=PT:
                            e.matmul(
                                ps2a[0:65, 512:1024], v[:, kt, 65:130],
                                PT[:, 1024 + kt2 * 512: 1024 + (kt2 + 1) * 512],
                                start=(ktp == 0 and kt2 == 0),
                                stop=(ktp == 7 and kt2 == 1)),
                            inc=([(sP, 1)] if kt2 == 1 else []))
                    PT_free[ptsl] = sP.n
                av_done = sP.n
                op("s", lambda e: e.activation(strip[64:65, 0:1024],
                                               ps2a[64:65, :], Act.Ln),
                   wait=[(sP, av_done), (sP, rep_guard_p)])
                op("s", lambda e: e.activation(strip[64:65, 0:1024],
                                               strip[64:65, 0:1024], Act.Exp,
                                               scale=-1.0), inc=[(sA, 1)])
                rec_done = sA.n
                for nh2 in range(2):
                    op("t", lambda e, nh2=nh2: e.matmul(
                        ps2b[0:65, nh2 * 512:(nh2 + 1) * 512],
                        ones66[64:65, 0:65],
                        strip[64:65, nh2 * 512:(nh2 + 1) * 512],
                        start=True, stop=True, tile_position=(64, 0)),
                        wait=([(sV, rep_guard_v), (sA, rec_done)]
                              if nh2 == 0 else []),
                        inc=([(sP, 1)] if nh2 == 1 else []))
                rep_guard_p = sP.n
                op("v", lambda e: e.tensor_copy(rbr[0:64, :], ps2b[0:64, :]),
                   wait=[(sP, rep_guard_p)], inc=[(sV, 1)])
                rep_guard_v = sV.n
                op("v", lambda e, p=p, qc=qc: e.tensor_tensor(
                    slabB[0:64, p * 2048 + qc * 512: p * 2048 + qc * 512 + 512],
                    ps2a[0:64, 0:512], rbr[0:64, 0:512], Alu.mult))
                hsl = hb_slot % 2
                op("v", lambda e, hsl=hsl: e.tensor_tensor(
                    hbnc[0:64, hsl * 512:(hsl + 1) * 512],
                    ps2a[0:64, 512:1024], rbr[0:64, 512:1024], Alu.mult),
                    wait=[(sHBL[hsl], hb_free[hsl])], inc=[(sV, 1)])
                av_guard_v = sV.n
                dma("y", sHB[hsl], hb_d[hsl], hbnc[0:64, hsl * 512:(hsl + 1) * 512],
                    wait=[(sV, av_guard_v)])
                wb = sHB[hsl].n
                dma("g", sHBL[hsl],
                    slabB[64:128, p * 2048 + qc * 512: p * 2048 + qc * 512 + 512],
                    hb_d[hsl], wait=[(sHB[hsl], wb)])
                hb_free[hsl] = sHBL[hsl].n
                hb_slot += 1
            qp_free[slq] = sP.n
            kp_free[slq] = sP.n
            vp_free[slq] = sP.n
        attn_done_hbl = [(sHBL[0], sHBL[0].n), (sHBL[1], sHBL[1].n)]
        attn_done_v = sV.n

        dma("g", sWKV, woTS.rearrange("p (c d2) -> p c d2", d2=1024),
            woT_d.rearrange("(c p) d2 -> p c d2", p=128),
            wait=[(sP, qkv_done_p)])
        woTS_g = sWKV.n
        oproj_evac_v = 0
        for tt in range(TT):
            pse = [ps2a, ps2b][tt % 2]
            for dc in range(DC):
                for nh2 in range(2):
                    w0 = []
                    if dc == 0 and nh2 == 0:
                        w0 = [(sV, attn_done_v if tt < 2 else oproj_evac_v)]
                        if tt == 0:
                            w0 += attn_done_hbl + [(sWKV, woTS_g),
                                                   (sV, attn_done_v)]
                    op("t", lambda e, tt=tt, dc=dc, nh2=nh2, p=pse: e.matmul(
                        p[:, nh2 * 512:(nh2 + 1) * 512],
                        slabB[:, dc * 2048 + tt * 128: dc * 2048 + tt * 128 + 128],
                        woTS[:, dc * 1024 + nh2 * 512: dc * 1024 + nh2 * 512 + 512],
                        start=(dc == 0), stop=(dc == DC - 1)),
                        wait=w0,
                        inc=([(sP, 1)] if (dc == DC - 1 and nh2 == 1) else []))
            mmv = sP.n
            sl = tt % 2
            dma("y", sX[sl], xb[sl][:, :], xr_d[tt],
                wait=[(sV, xb_free[sl]), (sOB[xrst[tt][0]], xrst[tt][1])])
            dx = sX[sl].n
            op("v", lambda e, o=ob[sl], p2=pse: e.tensor_tensor(
                o, p2[:, :], boS[:, :], Alu.add),
                wait=[(sP, mmv), (sX[sl], dx), (sOB[sl], ob_free[sl])] +
                     ([(sC, const_c)] if (s == 0 and tt == 0) else []))
            op("v", lambda e, o=ob[sl], b=xb[sl]: e.tensor_tensor(
                o, o, b[:, :], Alu.add), inc=[(sV, 1)])
            xb_free[sl] = sV.n
            oproj_evac_v = sV.n
            dma("y", sOB[sl], x1_d[tt], ob[sl], wait=[(sV, oproj_evac_v)])
            ob_free[sl] = sOB[sl].n
            x1st[tt] = (sl, sOB[sl].n)
        oproj_done_p = sP.n
        oproj_done_v = sV.n

        ln_slots = [xt, ob[0], ob[1]]
        for tt in range(TT):
            sl = tt % 2
            dma("y", sX1[sl], x1b[sl][:, :], x1_d[tt],
                wait=[(sV, x1b_free[sl]), (sOB[x1st[tt][0]], x1st[tt][1])])
            dx = sX1[sl].n
            lsl = tt % 3
            wn = [(sOB[lsl - 1], ob_free[lsl - 1])] if (tt in (1, 2)) else []
            ln_chain(x1b[sl][:, :], ln_slots[lsl][:, :],
                     wait_first=[(sX1[sl], dx)], wait_norm=wn)
            x1b_free[sl] = sV.n
            for ck in range(2):
                tr = []
                for j in range(4):
                    c = ck * 4 + j
                    extra = ([(sP, oproj_done_p)]
                             if (tt == 0 and ck == 0 and j == 0) else [])
                    tr.append((sTQ[qctr[0] % 2],
                               slabB[:, c * 2048 + tt * 128:
                                     c * 2048 + tt * 128 + 128], extra))
                q_push(ln_slots[lsl][:, ck * 512:(ck + 1) * 512], 512, tr=tr)
        q_flush()
        xqT = [(sTQ[0], sTQ[0].n), (sTQ[1], sTQ[1].n)]

        gu_evac_v = oproj_done_v
        silu_done = sA.n
        for tg in range(2):
            tg_start_p = sP.n
            for icb in range(ICB):
                dma("y", sWGU, wgu[:, 0:2048].rearrange("p (c w) -> p c w", w=256),
                    G_all[:, 0, :, icb * 256:(icb + 1) * 256]
                    .rearrange("c p w -> p c w"),
                    wait=[(sP, wgu_free)] +
                         ([(sCC, cc_done)] + xqT if (tg == 0 and icb < 1)
                          else []))
                dma("y", sWGU, wgu[:, 2048:4096].rearrange("p (c w) -> p c w", w=256),
                    G_all[:, 1, :, icb * 256:(icb + 1) * 256]
                    .rearrange("c p w -> p c w"))
                wld = sWGU.n
                for tt8 in range(8):
                    tt = tg * 8 + tt8
                    gb = (tt8 % 2) * 1024
                    for dc in range(DC):
                        w0 = []
                        if dc == 0:
                            w0 = [(sWGU, wld), (sV, gu_evac_v), (sA, silu_done),
                                  (sV, down_evac_v)]
                            if tg == 0 and icb == 0 and tt8 == 0:
                                w0 += [(sP, qkv_done_p), (sA, exp_done)]
                        op("t", lambda e, tt=tt, dc=dc, gb=gb: e.matmul(
                            psA[:, gb:gb + 256],
                            slabB[:, dc * 2048 + tt * 128:
                                  dc * 2048 + tt * 128 + 128],
                            wgu[:, dc * 256:(dc + 1) * 256],
                            start=(dc == 0), stop=(dc == DC - 1)), wait=w0)
                        op("t", lambda e, tt=tt, dc=dc, gb=gb: e.matmul(
                            psA[:, gb + 512:gb + 768],
                            slabB[:, dc * 2048 + tt * 128:
                                  dc * 2048 + tt * 128 + 128],
                            wgu[:, 2048 + dc * 256:2048 + (dc + 1) * 256],
                            start=(dc == 0), stop=(dc == DC - 1)),
                            inc=([(sP, 1)] if dc == DC - 1 else []))
                    guv = sP.n
                    op("s", lambda e, gb=gb: e.activation(
                        sgb[:, 0:256], psA[:, gb:gb + 256], Act.Silu),
                        wait=[(sP, guv)], inc=[(sA, 1)])
                    silu_done = sA.n
                    hsl = qctr[0] % 4
                    hh_ = strip[:, hsl * 256:(hsl + 1) * 256]
                    op("v", lambda e, gb=gb, hh_=hh_: e.tensor_tensor(
                        hh_, sgb[:, 0:256], psA[:, gb + 512:gb + 768],
                        Alu.mult), wait=[(sA, silu_done)], inc=[(sV, 1)])
                    gu_evac_v = sV.n
                    tr = []
                    for j in range(2):
                        c = icb * 2 + j
                        extra = ([(sP, tg_start_p)]
                                 if (icb == 0 and tt8 == 0 and j == 0) else [])
                        tr.append((sTH[qctr[0] % 2],
                                   slabA[:, c * 1024 + tt8 * 128:
                                         c * 1024 + tt8 * 128 + 128], extra))
                    q_push(hh_, 256, tr=tr)
                wgu_free = sP.n
            q_flush()
            hqT = [(sTH[0], sTH[0].n), (sTH[1], sTH[1].n)]
            for sub in range(2):
                psv = [psA[:, 0:1024], psA[:, 1024:2048], ps2a[:, :], ps2b[:, :]]
                for ic in range(ICN):
                    wsl = ic % 2
                    dma("y", sWD[wsl], wdb[:, wsl * 1024:(wsl + 1) * 1024]
                        .rearrange("q (r d) -> q r d", r=8),
                        G_all[:, 2, :, ic * 128:(ic + 1) * 128]
                        .rearrange("r q d -> q r d"),
                        wait=[(sP, wdb_free[wsl])] +
                             (hqT + [(sCC, cc_done)] if ic < 2 else []))
                    wld = sWD[wsl].n
                    for t4 in range(4):
                        tt8 = sub * 4 + t4
                        for nh2 in range(2):
                            w0 = []
                            if t4 == 0 and nh2 == 0:
                                w0 = [(sWD[wsl], wld)]
                                if ic == 0:
                                    w0 += [(sV, gu_evac_v), (sA, silu_done),
                                           (sV, down_evac_v if down_evac_v else
                                            oproj_done_v), (sV, attn_done_v)]
                            op("t", lambda e, ic=ic, tt8=tt8, t4=t4, nh2=nh2,
                                wsl=wsl: e.matmul(
                                    psv[t4][:, nh2 * 512:(nh2 + 1) * 512],
                                    slabA[:, ic * 1024 + tt8 * 128:
                                          ic * 1024 + tt8 * 128 + 128],
                                    wdb[:, wsl * 1024 + nh2 * 512:
                                        wsl * 1024 + nh2 * 512 + 512],
                                    start=(ic == 0), stop=(ic == ICN - 1)),
                                wait=w0,
                                inc=([(sP, 1)] if (t4 == 3 and nh2 == 1)
                                     else []))
                    wdb_free[wsl] = sP.n
                dn = sP.n
                for t4 in range(4):
                    tt = tg * 8 + sub * 4 + t4
                    sl = t4 % 2
                    dma("y", sX1[sl], x1b[sl][:, :], x1_d[tt],
                        wait=[(sV, x1b_free[sl]),
                              (sOB[x1st[tt][0]], x1st[tt][1])])
                    dx = sX1[sl].n
                    op("v", lambda e, o=ob[sl], pv=psv[t4], b=x1b[sl]:
                        e.tensor_tensor(o, pv, b[:, :], Alu.add),
                        wait=[(sP, dn), (sX1[sl], dx), (sOB[sl], ob_free[sl])],
                        inc=[(sV, 1)])
                    x1b_free[sl] = sV.n
                    down_evac_v = sV.n
                    dma("y", sOB[sl], out_d[s, tt * 128:(tt + 1) * 128, :],
                        ob[sl], wait=[(sV, down_evac_v)])
                    ob_free[sl] = sOB[sl].n
        if s == 0:
            wkv_ready = load_wqkv([(sP, sP.n)])

    fin0, fin1, fin_c = sOB[0].n, sOB[1].n, sC.n

    with nc.Block() as block:
        @block.tensor
        def _(t):
            for fn in ops["t"]:
                fn(t)

        @block.vector
        def _(v):
            for fn in ops["v"]:
                fn(v)

        @block.scalar
        def _(s_):
            for fn in ops["s"]:
                fn(s_)

        @block.gpsimd
        def _(g):
            for fn in ops["g"]:
                fn(g)

        @block.sync
        def _(y):
            for fn in ops["y"]:
                fn(y)
            y.wait_ge(sOB[0].h, fin0)
            y.wait_ge(sOB[1].h, fin1)
            y.wait_ge(sC.h, fin_c)

    es.close()
    return nc


def kernel(L0_attn_x, L0_attn_w_qkv, L0_attn_b_qkv, L0_attn_w_o, L0_attn_b_o,
           L0_ffn_w_gate, L0_ffn_w_up, L0_ffn_w_down):
    x = np.ascontiguousarray(np.asarray(L0_attn_x, dtype=np.float32))
    w_qkv = np.asarray(L0_attn_w_qkv, dtype=np.float32)
    b_qkv = np.asarray(L0_attn_b_qkv, dtype=np.float32)
    w_o = np.asarray(L0_attn_w_o, dtype=np.float32)
    b_o = np.asarray(L0_attn_b_o, dtype=np.float32)
    wg = np.ascontiguousarray(np.asarray(L0_ffn_w_gate, dtype=np.float32))
    wu = np.ascontiguousarray(np.asarray(L0_ffn_w_up, dtype=np.float32))
    wd = np.ascontiguousarray(np.asarray(L0_ffn_w_down, dtype=np.float32))

    wqkvT = np.ascontiguousarray(w_qkv.T)
    woT = np.ascontiguousarray(w_o.T)
    bqkv_qk = np.ascontiguousarray(b_qkv[:2048].reshape(16, 128).T)
    bv = np.ascontiguousarray(b_qkv[2048:].reshape(1, 1024))
    bo = np.ascontiguousarray(b_o.reshape(1, 1024))

    nc = build_nc()
    in_maps = []
    for i in range(8):
        in_maps.append({
            "x": np.ascontiguousarray(x[2 * i:2 * i + 2]),
            "wqkvT": wqkvT, "woT": woT, "bqkv": bqkv_qk, "bv": bv, "bo": bo,
            "wgs": np.ascontiguousarray(wg[:, i * 128:(i + 1) * 128]),
            "wus": np.ascontiguousarray(wu[:, i * 128:(i + 1) * 128]),
            "wds": np.ascontiguousarray(wd[i * 128:(i + 1) * 128, :]),
        })
    res = run_bass_kernel_spmd(nc, in_maps, core_ids=list(range(8)))
    return np.concatenate([np.asarray(r["out"]) for r in res.results], axis=0)


# version 06
# speedup vs baseline: 2.4381x; 2.4381x over previous
import numpy as np
import concourse.bass as bass
import concourse.mybir as mybir
from concourse.bass_utils import run_bass_kernel_spmd
from contextlib import ExitStack

F32 = mybir.dt.float32
BF16 = mybir.dt.bfloat16
Alu = mybir.AluOpType
Act = mybir.ActivationFunctionType
AX = mybir.AxisListType.X

D = 1024
L = 2048
NSEQ = 2
TT = 16
DC = 8
II = 2816
ICN = 22
ICB = 11
SCALE = 0.03125
EPS = 1e-5


class Sem:
    def __init__(self, h):
        self.h = h
        self.n = 0


def build_nc():
    import os
    nc = bass.Bass("TRN2", target_bir_lowering=False,
                   debug=(os.environ.get("KDBG") == "1"),
                   detect_race_conditions=False)

    x_d = nc.dram_tensor("x", [NSEQ, L, D], F32, kind="ExternalInput").ap()
    wqkvT_d = nc.dram_tensor("wqkvT", [D, 3 * D], F32, kind="ExternalInput").ap()
    woT_d = nc.dram_tensor("woT", [D, D], F32, kind="ExternalInput").ap()
    bqkv_d = nc.dram_tensor("bqkv", [128, 16], F32, kind="ExternalInput").ap()
    bv_d = nc.dram_tensor("bv", [1, D], F32, kind="ExternalInput").ap()
    bo_d = nc.dram_tensor("bo", [1, D], F32, kind="ExternalInput").ap()
    wgs_d = nc.dram_tensor("wgs", [II, 128], F32, kind="ExternalInput").ap()
    wus_d = nc.dram_tensor("wus", [II, 128], F32, kind="ExternalInput").ap()
    wds_d = nc.dram_tensor("wds", [128, II], F32, kind="ExternalInput").ap()
    out_d = nc.dram_tensor("out", [NSEQ, L, D], F32, kind="ExternalOutput").ap()
    wq_st = nc.dram_tensor("wq_st", [3, 128, II], BF16, kind="Internal").ap()
    G_all = nc.dram_tensor("G_all", [8, 3, 128, II], BF16, kind="Internal",
                           addr_space="Shared").ap()

    ktap = "ExternalOutput" if os.environ.get("KTAP") == "1" else "Internal"
    wgT_q = nc.dram_tensor("wgT_q", [D, II], BF16, kind=ktap).ap()
    wuT_q = nc.dram_tensor("wuT_q", [D, II], BF16, kind=ktap).ap()
    wdT_q = nc.dram_tensor("wdT_q", [II, D], BF16, kind=ktap).ap()
    qS_d = nc.dram_tensor("qS_d", [DC, 128, L], BF16, kind=ktap).ap()
    kS_d = nc.dram_tensor("kS_d", [DC, 128, L], BF16, kind=ktap).ap()
    vS_d = nc.dram_tensor("vS_d", [TT, 128, 16 * 65], BF16, kind=ktap).ap()
    x1_d = nc.dram_tensor("x1_d", [TT, 128, D], F32, kind=ktap).ap()
    xr_d = nc.dram_tensor("xr_d", [TT, 128, D], F32, kind=ktap).ap()
    hb_d = nc.dram_tensor("hb_d", [2, 64, 512], BF16).ap()

    es = ExitStack()
    sbt = lambda name, shape, dt: es.enter_context(nc.sbuf_tensor(name, shape, dt))
    psA = es.enter_context(nc.psum_tensor("psA", [128, 2048], F32))
    ps2a = es.enter_context(nc.psum_tensor("ps2a", [128, 1024], F32))
    ps2b = es.enter_context(nc.psum_tensor("ps2b", [128, 1024], F32))

    slabA = sbt("slabA", [128, 24576], BF16)
    slabB = sbt("slabB", [128, 16384], BF16)
    qp = sbt("qp", [128, 4096], BF16)
    kp = sbt("kp", [128, 4096], BF16)
    vp = sbt("vp", [128, 4160], BF16)
    wgu = sbt("wgu", [128, 4096], BF16)
    wdb = sbt("wdb", [128, 2048], BF16)
    PTb = sbt("PTb", [128, 4096], BF16)
    qtile = sbt("qtile", [128, 2048], BF16)
    qkstage = sbt("qkstage", [128, 4096], BF16)
    hbnc = sbt("hbnc", [128, 1024], BF16)

    qsc = sbt("qsc", [128, 3072], F32)
    n_, a_ = qsc[:, 0:512], qsc[:, 512:1024]
    acA, acB = qsc[:, 1024:1536], qsc[:, 1536:2048]
    sg_, ss_ = qsc[:, 2048:2560], qsc[:, 2560:3072]
    xsqv = qsc[:, 0:1024]
    amx3 = sbt("amx3", [128, 768], F32)

    def qslot_views(j):
        b = (j % 4) * 192
        return (amx3[:, b:b + 64], amx3[:, b + 64:b + 128],
                amx3[:, b + 128:b + 192])
    pool = sbt("pool", [128, 7168], F32)
    stat = sbt("stat", [128, 512], F32)
    st_sx = stat[:, 0:64]
    st_sq = stat[:, 64:128]
    st_mean = stat[:, 128:192]
    st_ex2 = stat[:, 192:256]
    st_m2 = stat[:, 256:320]
    st_var = stat[:, 320:384]
    st_va = stat[:, 384:448]
    st_rs = stat[:, 448:512]
    dmy = sbt("dmy", [128, 1024], F32)
    strip = sbt("strip", [128, 1024], F32)
    rbr = sbt("rbr", [128, 1024], F32)
    hh = strip[:, 0:512]
    sgb = rbr[:, 0:512]
    ones66 = sbt("ones66", [128, 66], F32)
    boS = sbt("boS", [128, 1024], F32)
    bvS = sbt("bvS", [128, 1024], F32)
    bqkvS = sbt("bqkvS", [128, 16], F32)

    xb = [pool[:, 0:1024], pool[:, 1024:2048]]
    x1b = [pool[:, 2048:3072], pool[:, 3072:4096]]
    ob = [pool[:, 4096:5120], pool[:, 5120:6144]]
    xt = pool[:, 6144:7168]
    woTS = slabA[:, 0:8192]

    sP = Sem(es.enter_context(nc.semaphore("sP")))
    sA = Sem(es.enter_context(nc.semaphore("sA")))
    sV = Sem(es.enter_context(nc.semaphore("sV")))
    names = ["sC", "sWF0", "sWF1", "sWF2", "sWF3", "sWST", "sX0", "sX1",
             "sOB0", "sOB1",
             "sQK", "sQK2", "sVS0", "sVS1", "sPQ0", "sPQ1", "sPK0", "sPK1",
             "sPV0", "sPV1", "sHB0", "sHB1", "sHBL0", "sHBL1",
             "sX10", "sX11", "sWGU", "sWD0", "sWD1",
             "sTQ0", "sTQ1", "sTH0", "sTH1", "sWKV", "sCC"]
    S = {nm: Sem(es.enter_context(nc.semaphore(nm))) for nm in names}
    sC = S["sC"]
    sWF = [S["sWF0"], S["sWF1"], S["sWF2"], S["sWF3"]]
    sCC = S["sCC"]
    sWST = S["sWST"]
    sX = [S["sX0"], S["sX1"]]
    sOB = [S["sOB0"], S["sOB1"]]
    sQK = S["sQK"]
    sQK2 = S["sQK2"]
    sVS = [S["sVS0"], S["sVS1"]]
    sPQ = [S["sPQ0"], S["sPQ1"]]
    sPK = [S["sPK0"], S["sPK1"]]
    sPV = [S["sPV0"], S["sPV1"]]
    sHB = [S["sHB0"], S["sHB1"]]
    sHBL = [S["sHBL0"], S["sHBL1"]]
    sX1 = [S["sX10"], S["sX11"]]
    sWGU = S["sWGU"]
    sWD = [S["sWD0"], S["sWD1"]]
    sTQ = [S["sTQ0"], S["sTQ1"]]
    sTH = [S["sTH0"], S["sTH1"]]
    sWKV = S["sWKV"]

    ops = {"t": [], "v": [], "s": [], "y": [], "g": []}

    def op(eng, fn, wait=(), inc=()):
        wait = [(s, v) for (s, v) in wait if v > 0]
        for s, v in wait:
            assert v <= s.n, f"wait on future value {v} > {s.n}"
        for s, k in inc:
            s.n += k
        vals = [(s.h, v) for (s, v) in wait]
        incs = [(s.h, k) for (s, k) in inc]

        def run(e):
            for h, v in vals:
                e.wait_ge(h, v)
            inst = fn(e)
            for h, k in incs:
                inst.then_inc(h, k)
        ops[eng].append(run)

    def dma(eng, sem, out, in_, wait=(), transpose=False):
        if transpose:
            op(eng, lambda e: e.dma_start_transpose(out, in_), wait=wait,
               inc=[(sem, 16)])
        else:
            op(eng, lambda e: e.dma_start(out=out, in_=in_), wait=wait,
               inc=[(sem, 16)])

    dma("y", sC, bqkvS[:, :], bqkv_d[:, :])
    dma("y", sC, bvS[:, :], bv_d.broadcast_to([128, D]))
    dma("y", sC, boS[:, :], bo_d.broadcast_to([128, D]))
    const_c = sC.n
    op("v", lambda e: e.memset(ones66[:, :], 1.0))
    op("v", lambda e: e.memset(amx3[:, :], 1.0))
    op("v", lambda e: e.memset(stat[:, :], 1.0))
    op("v", lambda e: e.memset(dmy[:, :], 0.0), inc=[(sV, 1)])

    qrec = []
    qctr = [0]
    slot_guard = {0: [], 1: [], 2: [], 3: []}

    def q_emit_ts(r):
        a, sc, _ = r["sv"]
        op("v", lambda e, sc=sc, a=a: e.tensor_scalar(
            sc[:, :], a[:, :], 1.0 / 6.0, 1e-12, Alu.mult, Alu.max))

    def q_emit_recip(r):
        _, sc, rr = r["sv"]
        op("v", lambda e, rr=rr, sc=sc: e.reciprocal(rr[:, :], sc[:, :]))

    def q_emit_apply(r):
        w, nb = r["w"], r["w"] // 16
        _, sc, rr = r["sv"]
        src3 = r["src"].rearrange("p (b k) -> p b k", k=16)
        rc3 = rr[:, 0:nb].unsqueeze(2).broadcast_to([128, nb, 16])
        sc3 = sc[:, 0:nb].unsqueeze(2).broadcast_to([128, nb, 16])
        nv = n_[:, 0:w]
        n3 = nv.rearrange("p (b k) -> p b k", k=16)
        op("v", lambda e, n3=n3, src3=src3, rc3=rc3: e.tensor_tensor(
            n3, src3, rc3, Alu.mult),
            wait=list(slot_guard[r["slot"]]), inc=[(sV, 1)])
        nready = sV.n
        op("s", lambda e, w=w, nv=nv: e.activation(sg_[:, 0:w], nv, Act.Sign),
           wait=[(sV, nready)], inc=[(sA, 1)])
        sgnready = sA.n
        av = a_[:, 0:w]
        op("s", lambda e, av=av, nv=nv: e.activation(av, nv, Act.Abs),
           inc=[(sA, 1)])
        absready = sA.n
        op("v", lambda e, av=av, w=w: e.tensor_scalar(
            acA[:, 0:w], av, 0.25, None, Alu.is_gt), wait=[(sA, absready)])
        for t in (0.75, 1.25, 1.75):
            op("v", lambda e, t=t, av=av, w=w: e.scalar_tensor_tensor(
                acA[:, 0:w], av, t, acA[:, 0:w], Alu.is_gt, Alu.add))
        op("v", lambda e, av=av, w=w: e.tensor_scalar(
            acB[:, 0:w], av, 5.0, 2.0, Alu.is_gt, Alu.mult))
        for t in (2.5, 3.5):
            op("v", lambda e, t=t, av=av, w=w: e.scalar_tensor_tensor(
                acB[:, 0:w], av, t, acB[:, 0:w], Alu.is_gt, Alu.add))
        op("v", lambda e, w=w: e.scalar_tensor_tensor(
            acA[:, 0:w], acA[:, 0:w], 0.5, acB[:, 0:w], Alu.mult, Alu.add))
        s3 = ss_[:, 0:w].rearrange("p (b k) -> p b k", k=16)
        sg3 = sg_[:, 0:w].rearrange("p (b k) -> p b k", k=16)
        op("v", lambda e, s3=s3, sg3=sg3, sc3=sc3: e.tensor_tensor(
            s3, sg3, sc3, Alu.mult), wait=[(sA, sgnready)])
        op("v", lambda e, dst=r["dst"], w=w: e.tensor_tensor(
            dst[:, 0:w], acA[:, 0:w], ss_[:, 0:w], Alu.mult), inc=[(sV, 1)])
        vready = sV.n
        flush_tr([(sV, vready)])
        tr = r["tr"]
        for j, (sem, dsl, extra) in enumerate(tr):
            pend_tr.append((sem, dsl, r["dst"][:, j * 128:(j + 1) * 128],
                            extra))
        if tr:
            slot_guard[r["slot"]] = [(tr[0][0], tr[0][0].n + len(tr) * 16)]
        if r["post"] is not None:
            r["post"](vready)

    def q_push(src, w, tr=(), wait_red=(), post=None):
        slot = qctr[0] % 4
        qctr[0] += 1
        sv = qslot_views(slot)
        nb = w // 16
        src3 = src.rearrange("p (b k) -> p b k", k=16)
        a = sv[0]
        op("v", lambda e, a=a, src3=src3, nb=nb: e.tensor_reduce(
            a[:, 0:nb], src3, AX, Alu.max, apply_absolute_value=True),
            wait=wait_red)
        qrec.append({"w": w, "sv": sv, "slot": slot, "src": src,
                     "dst": qtile[:, slot * 512: slot * 512 + 512],
                     "tr": list(tr), "post": post})
        if len(qrec) >= 2:
            q_emit_ts(qrec[-2])
        if len(qrec) >= 3:
            q_emit_recip(qrec[-3])
        if len(qrec) >= 4:
            q_emit_apply(qrec.pop(0))

    def q_flush():
        assert len(qrec) == 3, f"q_flush with {len(qrec)} pending"
        for _ in range(3):
            q_push(dmy[:, 0:256], 256)
        c0, c1, c2 = qrec
        q_emit_apply(c0)
        q_emit_ts(c2)
        q_emit_recip(c1)
        q_emit_apply(c1)
        q_emit_recip(c2)
        q_emit_apply(c2)
        qrec.clear()

    def ln_chain(srcb, dst, wait_first, wait_norm):
        src8 = srcb.rearrange("p (c w) -> p c w", w=128)
        op("s", lambda e: e.activation(xsqv, srcb, Act.Square),
           wait=list(wait_first) + [(sV, sV.n)], inc=[(sA, 1)])
        sqd = sA.n
        op("v", lambda e: e.tensor_reduce(st_sx[:, 0:1], srcb, AX, Alu.add),
           wait=wait_first)
        op("v", lambda e: e.tensor_reduce(st_sq[:, 0:1], xsqv, AX, Alu.add),
           wait=[(sA, sqd)])
        for k in (1, 2, 4):
            op("v", lambda e, k=k: e.tensor_copy(st_sx[:, k:2 * k], st_sx[:, 0:k]))
            op("v", lambda e, k=k: e.tensor_copy(st_sq[:, k:2 * k], st_sq[:, 0:k]))
        op("v", lambda e: e.tensor_scalar(st_mean, st_sx, 1.0 / D, None, Alu.mult))
        op("v", lambda e: e.tensor_scalar(st_ex2, st_sq, 1.0 / D, None, Alu.mult))
        op("v", lambda e: e.tensor_tensor(st_m2, st_mean, st_mean, Alu.mult))
        op("v", lambda e: e.tensor_tensor(st_var, st_ex2, st_m2, Alu.subtract))
        op("v", lambda e: e.tensor_scalar(st_va, st_var, EPS, None, Alu.add),
           inc=[(sV, 1)])
        vv = sV.n
        op("s", lambda e: e.activation(st_rs, st_va, Act.Ln), wait=[(sV, vv)])
        op("s", lambda e: e.activation(st_rs, st_rs, Act.Exp, scale=-0.5),
           inc=[(sA, 1)])
        aa = sA.n
        op("v", lambda e: e.engine_nop(), wait=[(sA, aa)] + wait_norm)
        mb = st_mean[:, 0:8].unsqueeze(2).broadcast_to([128, 8, 128])
        rb = st_rs[:, 0:8].unsqueeze(2).broadcast_to([128, 8, 128])
        xc8 = xsqv.rearrange("p (c w) -> p c w", w=128)
        op("v", lambda e: e.tensor_tensor(xc8, src8, mb, Alu.subtract))
        op("v", lambda e: e.tensor_tensor(
            dst.rearrange("p (c w) -> p c w", w=128), xc8, rb, Alu.mult),
            inc=[(sV, 1)])
        return sV.n

    pend_tr = []

    def flush_tr(wait_v):
        for i, (sem, dst, srcap, extra) in enumerate(pend_tr):
            dma("y", sem, dst, srcap,
                wait=(list(wait_v) + extra) if i == 0 else [], transpose=True)
        pend_tr.clear()

    def drain_dummy():
        op("v", lambda e: e.tensor_copy(dmy[:, 0:512], dmy[:, 512:1024]),
           inc=[(sV, 1)])
        return sV.n

    wq_buf_free = [0, 0, 0, 0]
    wfs = [pool[:, 0:128], pool[:, 128:256], pool[:, 256:384],
           pool[:, 384:512]]
    dma("y", sWF[0], pool[:, 3072:5888], wds_d[:, :])
    wd_dv = sWF[0].n
    wti = 0
    for mi, (src_d, strip0) in enumerate([(wgs_d, 0), (wus_d, 2816)]):
        for it in range(ICN):
            sl = wti % 4
            wti += 1
            dma("y", sWF[sl], wfs[sl][:, :], src_d[it * 128:(it + 1) * 128, :],
                wait=[(sV, wq_buf_free[sl])])
            dv = sWF[sl].n
            tr = [(sTQ[qctr[0] % 2],
                   slabA[:, strip0 + it * 128: strip0 + it * 128 + 128], [])]

            def post(vready, sl=sl):
                wq_buf_free[sl] = sV.n
            q_push(wfs[sl][:, :], 128, tr=tr, wait_red=[(sWF[sl], dv)],
                   post=post)
    for ck in range(6):
        c0 = ck * 512
        w = min(512, II - c0)
        tr = []
        for j in range(w // 128):
            c = (c0 + j * 128) // 128
            tr.append((sTQ[qctr[0] % 2],
                       slabA[:, 5632 + c * 128: 5632 + c * 128 + 128], []))
        q_push(pool[:, 3072 + c0:3072 + c0 + w], w, tr=tr,
               wait_red=[(sWF[0], wd_dv)])
    q_flush()
    tqw = [(sTQ[0], sTQ[0].n), (sTQ[1], sTQ[1].n)]
    dma("y", sWST, wq_st[0], slabA[:, 0:2816], wait=tqw)
    dma("y", sWST, wq_st[1], slabA[:, 2816:5632])
    dma("y", sWST, wq_st[2], slabA[:, 5632:8448])
    w_done = sWST.n
    w_quant_v = sV.n

    def emit_collective():
        op("g", lambda e: e.collective_compute(
            "AllGather", Alu.bypass, [list(range(8))],
            ins=[wq_st[:, :, :]],
            outs=[G_all[:, :, :, :]]),
            wait=[(sWST, w_done)], inc=[(sCC, 1)])
        cc = sCC.n
        if os.environ.get("KTAP") == "1":
            dma("y", sC, wgT_q.rearrange("(r p) w -> r p w", r=8),
                G_all[:, 0, :, :], wait=[(sCC, cc)])
            dma("y", sC, wuT_q.rearrange("(r p) w -> r p w", r=8),
                G_all[:, 1, :, :])
            for rr in range(8):
                dma("y", sC,
                    wdT_q[:, rr * 128:(rr + 1) * 128]
                    .rearrange("(c q) d -> c q d", q=128),
                    G_all[rr, 2, :, :].rearrange("q (c d) -> c q d", d=128))
        return cc

    xb_free = [0, 0]
    x1b_free = [0, 0]
    ob_free = [0, 0]
    qk_free = [0, 0]
    hb_free = [0, 0]
    wgu_free = 0
    wdb_free = [0, 0]
    qp_free = [0, 0]
    kp_free = [0, 0]
    vp_free = [0, 0]
    PT_free = [0, 0]
    vst_free = [0, 0]
    xrst = [None] * TT
    x1st = [None] * TT

    def load_wqkv(wait0):
        for dc in range(DC):
            dma("g", sWKV, slabA[:, dc * 3072:(dc + 1) * 3072],
                wqkvT_d[dc * 128:(dc + 1) * 128, :],
                wait=(wait0 if dc == 0 else []))
        return sWKV.n

    wkv_ready = load_wqkv([(sWST, w_done)])
    cc_done = emit_collective()
    exp_done = 0
    down_evac_v = 0

    for s in range(NSEQ):
        seq_start_p = sP.n

        for tt in range(TT):
            sl = tt % 2
            dma("y", sX[sl], xb[sl][:, :], x_d[s, tt * 128:(tt + 1) * 128, :],
                wait=[(sV, xb_free[sl])] +
                     ([(sV, w_quant_v), (sWST, w_done)]
                      if (s == 0 and tt == 0) else []))
            dx = sX[sl].n
            qsl = tt % 2
            qt_ = qtile[:, qsl * 1024:(qsl + 1) * 1024]
            vq = ln_chain(xb[sl][:, :], qt_,
                          wait_first=[(sX[sl], dx)],
                          wait_norm=slot_guard[2 * qsl] +
                                    slot_guard[2 * qsl + 1])
            op("v", lambda e, o=ob[sl], b=xb[sl], q=qt_: e.tensor_tensor(
                o, b[:, :], q, Alu.add),
                wait=[(sOB[sl], ob_free[sl])], inc=[(sV, 1)])
            xb_free[sl] = sV.n
            dma("y", sOB[sl], xr_d[tt], ob[sl], wait=[(sV, sV.n)])
            ob_free[sl] = sOB[sl].n
            xrst[tt] = (sl, sOB[sl].n)
            flush_tr([(sV, vq)])
            for c in range(DC):
                extra = [(sP, seq_start_p)] if (tt == 0 and c == 0) else []
                pend_tr.append((sTQ[qsl],
                                slabB[:, c * 2048 + tt * 128:
                                      c * 2048 + tt * 128 + 128],
                                qt_[:, c * 128:(c + 1) * 128], extra))
            slot_guard[2 * qsl] = [(sTQ[qsl], sTQ[qsl].n + 8 * 16)]
            slot_guard[2 * qsl + 1] = [(sTQ[qsl], sTQ[qsl].n + 8 * 16)]
        flush_tr([(sV, drain_dummy())])
        xnT0, xnT1 = sTQ[0].n, sTQ[1].n

        evac_prev = [0, 0, 0, 0]
        for ec in range(2 * DC):
            ssl = ec % 2
            stg = qkstage[:, ssl * 2048:(ssl + 1) * 2048]
            sQKs = [sQK, sQK2][ssl]
            for qt in range(4):
                for dc in range(DC):
                    w0 = []
                    if dc == 0:
                        w0 = [(sA, evac_prev[qt])]
                        if qt == 0:
                            w0 += [(sV, sV.n)]
                        if ec == 0 and qt == 0:
                            w0 += [(sWKV, wkv_ready), (sTQ[0], xnT0),
                                   (sTQ[1], xnT1)]
                    op("t", lambda e, ec=ec, dc=dc, qt=qt: e.matmul(
                        psA[:, qt * 512:(qt + 1) * 512],
                        slabA[:, dc * 3072 + ec * 128: dc * 3072 + ec * 128 + 128],
                        slabB[:, dc * 2048 + qt * 512: dc * 2048 + qt * 512 + 512],
                        start=(dc == 0), stop=(dc == DC - 1)),
                        wait=w0,
                        inc=([(sP, 1)] if dc == DC - 1 else []))
                mmv = sP.n
                op("s", lambda e, ec=ec, qt=qt, stg=stg: e.activation(
                    stg[:, qt * 512:(qt + 1) * 512],
                    psA[:, qt * 512:(qt + 1) * 512],
                    Act.Identity, bias=bqkvS[:, ec:ec + 1]),
                    wait=[(sP, mmv)] +
                         ([(sQKs, qk_free[ssl])] if qt == 0 else []) +
                         ([(sC, const_c)] if (s == 0 and ec == 0 and qt == 0)
                          else []),
                    inc=[(sA, 1)])
                evac_prev[qt] = sA.n
            ev = sA.n
            dst = (qS_d if ec < DC else kS_d)[ec % DC]
            dma("y", sQKs, dst[:, :], stg, wait=[(sA, ev)])
            qk_free[ssl] = sQKs.n

        for tt in range(TT):
            pse = [ps2a, ps2b][tt % 2]
            for dc in range(DC):
                for nh2 in range(2):
                    w0 = []
                    if dc == 0 and nh2 == 0:
                        w0 = [(sV, sV.n)]
                    op("t", lambda e, tt=tt, dc=dc, nh2=nh2, p=pse: e.matmul(
                        p[:, nh2 * 512:(nh2 + 1) * 512],
                        slabB[:, dc * 2048 + tt * 128: dc * 2048 + tt * 128 + 128],
                        slabA[:, dc * 3072 + 2048 + nh2 * 512:
                              dc * 3072 + 2048 + nh2 * 512 + 512],
                        start=(dc == 0), stop=(dc == DC - 1)),
                        wait=w0,
                        inc=([(sP, 1)] if (dc == DC - 1 and nh2 == 1) else []))
            mmv = sP.n
            vsl = tt % 2
            vstage = PTb[:, vsl * 2048:vsl * 2048 + 1040]
            v3 = vstage.rearrange("p (h c) -> p h c", c=65)
            op("v", lambda e, v3=v3: e.memset(v3[:, :, 64:65], 1.0),
               wait=[(sP, mmv), (sVS[vsl], vst_free[vsl])] +
                    ([(sC, const_c)] if (s == 0 and tt == 0) else []))
            op("v", lambda e, v3=v3, p=pse: e.tensor_tensor(
                v3[:, :, 0:64], p[:, :].rearrange("p (h c) -> p h c", c=64),
                bvS[:, :].rearrange("p (h c) -> p h c", c=64), Alu.add),
                inc=[(sV, 1)])
            dma("y", sVS[vsl], vS_d[tt], vstage, wait=[(sV, sV.n)])
            vst_free[vsl] = sVS[vsl].n
        stage_done = [(sQK, sQK.n), (sQK2, sQK2.n),
                      (sVS[0], sVS[0].n), (sVS[1], sVS[1].n)]
        qkv_done_p = sP.n
        qkv_done_v = sV.n

        av_guard_v = qkv_done_v
        rep_guard_v = qkv_done_v
        rep_guard_p = sP.n
        if exp_done == 0:
            exp_done = sA.n
        hb_slot = 0
        first_exp = True
        for p in range(DC):
            slq = p % 2
            dma("y", sPQ[slq], qp[:, slq * 2048:(slq + 1) * 2048], qS_d[p],
                wait=[(sP, qp_free[slq])] + (stage_done if p < 2 else []))
            dma("y", sPK[slq], kp[:, slq * 2048:(slq + 1) * 2048], kS_d[p],
                wait=[(sP, kp_free[slq])])
            for g4 in range(4):
                dma("y", sPV[slq],
                    vp[:, slq * 2080 + g4 * 520: slq * 2080 + (g4 + 1) * 520]
                    .rearrange("p (t c) -> p t c", c=130),
                    vS_d[g4 * 4:(g4 + 1) * 4, :, p * 130:(p + 1) * 130]
                    .transpose([1, 0, 2]),
                    wait=[(sP, vp_free[slq])] if g4 == 0 else [])
            pair_ld = [(sPQ[slq], sPQ[slq].n), (sPK[slq], sPK[slq].n),
                       (sPV[slq], sPV[slq].n)]
            qpb = qp[:, slq * 2048:(slq + 1) * 2048]
            kpb = kp[:, slq * 2048:(slq + 1) * 2048]
            vpb = vp[:, slq * 2080:(slq + 1) * 2080].rearrange(
                "p (t c) -> p t c", c=130)
            for qc in range(4):
                for ktp in range(8):
                    ptsl = ktp % 2
                    PT = PTb[:, ptsl * 2048:(ptsl + 1) * 2048]
                    for kt2 in range(2):
                        kt = ktp * 2 + kt2
                        w0 = []
                        if kt2 == 0:
                            w0 = [(sA, exp_done)]
                            if qc == 0 and ktp == 0:
                                w0 += pair_ld
                        op("t", lambda e, kt=kt, kt2=kt2, q=qpb, k=kpb, qc=qc:
                            e.matmul(
                                psA[:, kt2 * 512:(kt2 + 1) * 512],
                                k[0:64, kt * 128:(kt + 1) * 128],
                                q[0:64, qc * 512:(qc + 1) * 512],
                                start=True, stop=True, tile_position=(0, 0)),
                            wait=w0)
                        op("t", lambda e, kt=kt, kt2=kt2, q=qpb, k=kpb, qc=qc:
                            e.matmul(
                                psA[:, 1024 + kt2 * 512: 1024 + (kt2 + 1) * 512],
                                k[64:128, kt * 128:(kt + 1) * 128],
                                q[64:128, qc * 512:(qc + 1) * 512],
                                start=True, stop=True, tile_position=(64, 0)),
                            inc=([(sP, 1)] if kt2 == 1 else []))
                    sc_done = sP.n
                    ew = [(sP, sc_done), (sP, PT_free[ptsl])]
                    if first_exp:
                        ew += stage_done
                        if ktp == 1:
                            first_exp = False
                    op("s", lambda e, PT=PT: e.activation(PT, psA[:, :], Act.Exp,
                                                          scale=SCALE),
                       wait=ew, inc=[(sA, 1)])
                    exp_done = sA.n
                    for kt2 in range(2):
                        kt = ktp * 2 + kt2
                        w0 = []
                        if kt2 == 0:
                            w0 = [(sA, exp_done)]
                            if ktp == 0:
                                w0 += [(sV, av_guard_v)]
                        op("t", lambda e, kt=kt, kt2=kt2, ktp=ktp, v=vpb, PT=PT:
                            e.matmul(
                                ps2a[0:65, 0:512], v[:, kt, 0:65],
                                PT[:, kt2 * 512:(kt2 + 1) * 512],
                                start=(ktp == 0 and kt2 == 0),
                                stop=(ktp == 7 and kt2 == 1)),
                            wait=w0)
                        op("t", lambda e, kt=kt, kt2=kt2, ktp=ktp, v=vpb, PT=PT:
                            e.matmul(
                                ps2a[0:65, 512:1024], v[:, kt, 65:130],
                                PT[:, 1024 + kt2 * 512: 1024 + (kt2 + 1) * 512],
                                start=(ktp == 0 and kt2 == 0),
                                stop=(ktp == 7 and kt2 == 1)),
                            inc=([(sP, 1)] if kt2 == 1 else []))
                    PT_free[ptsl] = sP.n
                av_done = sP.n
                op("s", lambda e: e.activation(strip[64:65, 0:1024],
                                               ps2a[64:65, :], Act.Ln),
                   wait=[(sP, av_done), (sP, rep_guard_p)])
                op("s", lambda e: e.activation(strip[64:65, 0:1024],
                                               strip[64:65, 0:1024], Act.Exp,
                                               scale=-1.0), inc=[(sA, 1)])
                rec_done = sA.n
                for nh2 in range(2):
                    op("t", lambda e, nh2=nh2: e.matmul(
                        ps2b[0:65, nh2 * 512:(nh2 + 1) * 512],
                        ones66[64:65, 0:65],
                        strip[64:65, nh2 * 512:(nh2 + 1) * 512],
                        start=True, stop=True, tile_position=(64, 0)),
                        wait=([(sV, rep_guard_v), (sA, rec_done)]
                              if nh2 == 0 else []),
                        inc=([(sP, 1)] if nh2 == 1 else []))
                rep_guard_p = sP.n
                op("v", lambda e: e.tensor_copy(rbr[0:64, :], ps2b[0:64, :]),
                   wait=[(sP, rep_guard_p)], inc=[(sV, 1)])
                rep_guard_v = sV.n
                op("v", lambda e, p=p, qc=qc: e.tensor_tensor(
                    slabB[0:64, p * 2048 + qc * 512: p * 2048 + qc * 512 + 512],
                    ps2a[0:64, 0:512], rbr[0:64, 0:512], Alu.mult))
                hsl = hb_slot % 2
                op("v", lambda e, hsl=hsl: e.tensor_tensor(
                    hbnc[0:64, hsl * 512:(hsl + 1) * 512],
                    ps2a[0:64, 512:1024], rbr[0:64, 512:1024], Alu.mult),
                    wait=[(sHBL[hsl], hb_free[hsl])], inc=[(sV, 1)])
                av_guard_v = sV.n
                dma("y", sHB[hsl], hb_d[hsl], hbnc[0:64, hsl * 512:(hsl + 1) * 512],
                    wait=[(sV, av_guard_v)])
                wb = sHB[hsl].n
                dma("g", sHBL[hsl],
                    slabB[64:128, p * 2048 + qc * 512: p * 2048 + qc * 512 + 512],
                    hb_d[hsl], wait=[(sHB[hsl], wb)])
                hb_free[hsl] = sHBL[hsl].n
                hb_slot += 1
            qp_free[slq] = sP.n
            kp_free[slq] = sP.n
            vp_free[slq] = sP.n
        attn_done_hbl = [(sHBL[0], sHBL[0].n), (sHBL[1], sHBL[1].n)]
        attn_done_v = sV.n

        dma("g", sWKV, woTS.rearrange("p (c d2) -> p c d2", d2=1024),
            woT_d.rearrange("(c p) d2 -> p c d2", p=128),
            wait=[(sP, qkv_done_p)])
        woTS_g = sWKV.n
        oproj_evac_v = 0
        for tt in range(TT):
            pse = [ps2a, ps2b][tt % 2]
            for dc in range(DC):
                for nh2 in range(2):
                    w0 = []
                    if dc == 0 and nh2 == 0:
                        w0 = [(sV, attn_done_v if tt < 2 else oproj_evac_v)]
                        if tt == 0:
                            w0 += attn_done_hbl + [(sWKV, woTS_g),
                                                   (sV, attn_done_v)]
                    op("t", lambda e, tt=tt, dc=dc, nh2=nh2, p=pse: e.matmul(
                        p[:, nh2 * 512:(nh2 + 1) * 512],
                        slabB[:, dc * 2048 + tt * 128: dc * 2048 + tt * 128 + 128],
                        woTS[:, dc * 1024 + nh2 * 512: dc * 1024 + nh2 * 512 + 512],
                        start=(dc == 0), stop=(dc == DC - 1)),
                        wait=w0,
                        inc=([(sP, 1)] if (dc == DC - 1 and nh2 == 1) else []))
            mmv = sP.n
            sl = tt % 2
            dma("y", sX[sl], xb[sl][:, :], xr_d[tt],
                wait=[(sV, xb_free[sl]), (sOB[xrst[tt][0]], xrst[tt][1])])
            dx = sX[sl].n
            op("v", lambda e, o=ob[sl], p2=pse: e.tensor_tensor(
                o, p2[:, :], boS[:, :], Alu.add),
                wait=[(sP, mmv), (sX[sl], dx), (sOB[sl], ob_free[sl])] +
                     ([(sC, const_c)] if (s == 0 and tt == 0) else []))
            op("v", lambda e, o=ob[sl], b=xb[sl]: e.tensor_tensor(
                o, o, b[:, :], Alu.add), inc=[(sV, 1)])
            xb_free[sl] = sV.n
            oproj_evac_v = sV.n
            dma("y", sOB[sl], x1_d[tt], ob[sl], wait=[(sV, oproj_evac_v)])
            ob_free[sl] = sOB[sl].n
            x1st[tt] = (sl, sOB[sl].n)
        oproj_done_p = sP.n
        oproj_done_v = sV.n

        ln_slots = [xt, ob[0], ob[1]]
        for tt in range(TT):
            sl = tt % 2
            dma("y", sX1[sl], x1b[sl][:, :], x1_d[tt],
                wait=[(sV, x1b_free[sl]), (sOB[x1st[tt][0]], x1st[tt][1])])
            dx = sX1[sl].n
            lsl = tt % 3
            wn = [(sOB[lsl - 1], ob_free[lsl - 1])] if (tt in (1, 2)) else []
            ln_chain(x1b[sl][:, :], ln_slots[lsl][:, :],
                     wait_first=[(sX1[sl], dx)], wait_norm=wn)
            x1b_free[sl] = sV.n
            for ck in range(2):
                tr = []
                for j in range(4):
                    c = ck * 4 + j
                    extra = ([(sP, oproj_done_p)]
                             if (tt == 0 and ck == 0 and j == 0) else [])
                    tr.append((sTQ[qctr[0] % 2],
                               slabB[:, c * 2048 + tt * 128:
                                     c * 2048 + tt * 128 + 128], extra))
                q_push(ln_slots[lsl][:, ck * 512:(ck + 1) * 512], 512, tr=tr)
        q_flush()
        xqT = [(sTQ[0], sTQ[0].n), (sTQ[1], sTQ[1].n)]

        gu_evac_v = oproj_done_v
        silu_done = sA.n
        for tg in range(2):
            tg_start_p = sP.n
            for icb in range(ICB):
                dma("y", sWGU, wgu[:, 0:2048].rearrange("p (c w) -> p c w", w=256),
                    G_all[:, 0, :, icb * 256:(icb + 1) * 256]
                    .rearrange("c p w -> p c w"),
                    wait=[(sP, wgu_free)] +
                         ([(sCC, cc_done)] + xqT if (tg == 0 and icb < 1)
                          else []))
                dma("y", sWGU, wgu[:, 2048:4096].rearrange("p (c w) -> p c w", w=256),
                    G_all[:, 1, :, icb * 256:(icb + 1) * 256]
                    .rearrange("c p w -> p c w"))
                wld = sWGU.n
                for tt8 in range(8):
                    tt = tg * 8 + tt8
                    gb = (tt8 % 2) * 1024
                    for dc in range(DC):
                        w0 = []
                        if dc == 0:
                            w0 = [(sWGU, wld), (sV, gu_evac_v), (sA, silu_done),
                                  (sV, down_evac_v)]
                            if tg == 0 and icb == 0 and tt8 == 0:
                                w0 += [(sP, qkv_done_p), (sA, exp_done)]
                        op("t", lambda e, tt=tt, dc=dc, gb=gb: e.matmul(
                            psA[:, gb:gb + 256],
                            slabB[:, dc * 2048 + tt * 128:
                                  dc * 2048 + tt * 128 + 128],
                            wgu[:, dc * 256:(dc + 1) * 256],
                            start=(dc == 0), stop=(dc == DC - 1)), wait=w0)
                        op("t", lambda e, tt=tt, dc=dc, gb=gb: e.matmul(
                            psA[:, gb + 512:gb + 768],
                            slabB[:, dc * 2048 + tt * 128:
                                  dc * 2048 + tt * 128 + 128],
                            wgu[:, 2048 + dc * 256:2048 + (dc + 1) * 256],
                            start=(dc == 0), stop=(dc == DC - 1)),
                            inc=([(sP, 1)] if dc == DC - 1 else []))
                    guv = sP.n
                    op("s", lambda e, gb=gb: e.activation(
                        sgb[:, 0:256], psA[:, gb:gb + 256], Act.Silu),
                        wait=[(sP, guv)], inc=[(sA, 1)])
                    silu_done = sA.n
                    hsl = qctr[0] % 4
                    hh_ = strip[:, hsl * 256:(hsl + 1) * 256]
                    op("v", lambda e, gb=gb, hh_=hh_: e.tensor_tensor(
                        hh_, sgb[:, 0:256], psA[:, gb + 512:gb + 768],
                        Alu.mult), wait=[(sA, silu_done)], inc=[(sV, 1)])
                    gu_evac_v = sV.n
                    tr = []
                    for j in range(2):
                        c = icb * 2 + j
                        extra = ([(sP, tg_start_p)]
                                 if (icb == 0 and tt8 == 0 and j == 0) else [])
                        tr.append((sTH[qctr[0] % 2],
                                   slabA[:, c * 1024 + tt8 * 128:
                                         c * 1024 + tt8 * 128 + 128], extra))
                    q_push(hh_, 256, tr=tr)
                wgu_free = sP.n
            q_flush()
            hqT = [(sTH[0], sTH[0].n), (sTH[1], sTH[1].n)]
            for sub in range(2):
                psv = [psA[:, 0:1024], psA[:, 1024:2048], ps2a[:, :], ps2b[:, :]]
                for ic in range(ICN):
                    wsl = ic % 2
                    dma("y", sWD[wsl], wdb[:, wsl * 1024:(wsl + 1) * 1024]
                        .rearrange("q (r d) -> q r d", r=8),
                        G_all[:, 2, :, ic * 128:(ic + 1) * 128]
                        .rearrange("r q d -> q r d"),
                        wait=[(sP, wdb_free[wsl])] +
                             (hqT + [(sCC, cc_done)] if ic < 2 else []))
                    wld = sWD[wsl].n
                    for t4 in range(4):
                        tt8 = sub * 4 + t4
                        for nh2 in range(2):
                            w0 = []
                            if t4 == 0 and nh2 == 0:
                                w0 = [(sWD[wsl], wld)]
                                if ic == 0:
                                    w0 += [(sV, gu_evac_v), (sA, silu_done),
                                           (sV, down_evac_v if down_evac_v else
                                            oproj_done_v), (sV, attn_done_v)]
                            op("t", lambda e, ic=ic, tt8=tt8, t4=t4, nh2=nh2,
                                wsl=wsl: e.matmul(
                                    psv[t4][:, nh2 * 512:(nh2 + 1) * 512],
                                    slabA[:, ic * 1024 + tt8 * 128:
                                          ic * 1024 + tt8 * 128 + 128],
                                    wdb[:, wsl * 1024 + nh2 * 512:
                                        wsl * 1024 + nh2 * 512 + 512],
                                    start=(ic == 0), stop=(ic == ICN - 1)),
                                wait=w0,
                                inc=([(sP, 1)] if (t4 == 3 and nh2 == 1)
                                     else []))
                    wdb_free[wsl] = sP.n
                dn = sP.n
                for t4 in range(4):
                    tt = tg * 8 + sub * 4 + t4
                    sl = t4 % 2
                    dma("y", sX1[sl], x1b[sl][:, :], x1_d[tt],
                        wait=[(sV, x1b_free[sl]),
                              (sOB[x1st[tt][0]], x1st[tt][1])])
                    dx = sX1[sl].n
                    op("v", lambda e, o=ob[sl], pv=psv[t4], b=x1b[sl]:
                        e.tensor_tensor(o, pv, b[:, :], Alu.add),
                        wait=[(sP, dn), (sX1[sl], dx), (sOB[sl], ob_free[sl])],
                        inc=[(sV, 1)])
                    x1b_free[sl] = sV.n
                    down_evac_v = sV.n
                    dma("y", sOB[sl], out_d[s, tt * 128:(tt + 1) * 128, :],
                        ob[sl], wait=[(sV, down_evac_v)])
                    ob_free[sl] = sOB[sl].n
        if s == 0:
            wkv_ready = load_wqkv([(sP, sP.n)])

    fin0, fin1, fin_c = sOB[0].n, sOB[1].n, sC.n

    with nc.Block() as block:
        @block.tensor
        def _(t):
            for fn in ops["t"]:
                fn(t)

        @block.vector
        def _(v):
            for fn in ops["v"]:
                fn(v)

        @block.scalar
        def _(s_):
            for fn in ops["s"]:
                fn(s_)

        @block.gpsimd
        def _(g):
            for fn in ops["g"]:
                fn(g)

        @block.sync
        def _(y):
            for fn in ops["y"]:
                fn(y)
            y.wait_ge(sOB[0].h, fin0)
            y.wait_ge(sOB[1].h, fin1)
            y.wait_ge(sC.h, fin_c)

    es.close()
    return nc


def kernel(L0_attn_x, L0_attn_w_qkv, L0_attn_b_qkv, L0_attn_w_o, L0_attn_b_o,
           L0_ffn_w_gate, L0_ffn_w_up, L0_ffn_w_down):
    x = np.ascontiguousarray(np.asarray(L0_attn_x, dtype=np.float32))
    w_qkv = np.asarray(L0_attn_w_qkv, dtype=np.float32)
    b_qkv = np.asarray(L0_attn_b_qkv, dtype=np.float32)
    w_o = np.asarray(L0_attn_w_o, dtype=np.float32)
    b_o = np.asarray(L0_attn_b_o, dtype=np.float32)
    wg = np.ascontiguousarray(np.asarray(L0_ffn_w_gate, dtype=np.float32))
    wu = np.ascontiguousarray(np.asarray(L0_ffn_w_up, dtype=np.float32))
    wd = np.ascontiguousarray(np.asarray(L0_ffn_w_down, dtype=np.float32))

    wqkvT = np.ascontiguousarray(w_qkv.T)
    woT = np.ascontiguousarray(w_o.T)
    bqkv_qk = np.ascontiguousarray(b_qkv[:2048].reshape(16, 128).T)
    bv = np.ascontiguousarray(b_qkv[2048:].reshape(1, 1024))
    bo = np.ascontiguousarray(b_o.reshape(1, 1024))

    nc = build_nc()
    in_maps = []
    for i in range(8):
        in_maps.append({
            "x": np.ascontiguousarray(x[2 * i:2 * i + 2]),
            "wqkvT": wqkvT, "woT": woT, "bqkv": bqkv_qk, "bv": bv, "bo": bo,
            "wgs": np.ascontiguousarray(wg[:, i * 128:(i + 1) * 128]),
            "wus": np.ascontiguousarray(wu[:, i * 128:(i + 1) * 128]),
            "wds": np.ascontiguousarray(wd[i * 128:(i + 1) * 128, :]),
        })
    res = run_bass_kernel_spmd(nc, in_maps, core_ids=list(range(8)))
    return np.concatenate([np.asarray(r["out"]) for r in res.results], axis=0)


# version 07
# speedup vs baseline: 2.5042x; 1.0271x over previous
import numpy as np
import concourse.bass as bass
import concourse.mybir as mybir
from concourse.bass_utils import run_bass_kernel_spmd
from contextlib import ExitStack

F32 = mybir.dt.float32
BF16 = mybir.dt.bfloat16
Alu = mybir.AluOpType
Act = mybir.ActivationFunctionType
AX = mybir.AxisListType.X

D = 1024
L = 2048
NSEQ = 2
TT = 16
DC = 8
II = 2816
ICN = 22
ICB = 11
SCALE = 0.03125
EPS = 1e-5


class Sem:
    def __init__(self, h):
        self.h = h
        self.n = 0


def build_nc():
    import os
    nc = bass.Bass("TRN2", target_bir_lowering=False,
                   debug=(os.environ.get("KDBG") == "1"),
                   detect_race_conditions=False)

    x_d = nc.dram_tensor("x", [NSEQ, L, D], F32, kind="ExternalInput").ap()
    wqkvT_d = nc.dram_tensor("wqkvT", [D, 3 * D], F32, kind="ExternalInput").ap()
    woT_d = nc.dram_tensor("woT", [D, D], F32, kind="ExternalInput").ap()
    bqkv_d = nc.dram_tensor("bqkv", [128, 16], F32, kind="ExternalInput").ap()
    bv_d = nc.dram_tensor("bv", [1, D], F32, kind="ExternalInput").ap()
    bo_d = nc.dram_tensor("bo", [1, D], F32, kind="ExternalInput").ap()
    wgs_d = nc.dram_tensor("wgs", [II, 128], F32, kind="ExternalInput").ap()
    wus_d = nc.dram_tensor("wus", [II, 128], F32, kind="ExternalInput").ap()
    wds_d = nc.dram_tensor("wds", [128, II], F32, kind="ExternalInput").ap()
    out_d = nc.dram_tensor("out", [NSEQ, L, D], F32, kind="ExternalOutput").ap()
    wq_st = nc.dram_tensor("wq_st", [3, 128, II], BF16, kind="Internal").ap()
    G_all = nc.dram_tensor("G_all", [8, 3, 128, II], BF16, kind="Internal",
                           addr_space="Shared").ap()

    ktap = "ExternalOutput" if os.environ.get("KTAP") == "1" else "Internal"
    wgT_q = nc.dram_tensor("wgT_q", [D, II], BF16, kind=ktap).ap()
    wuT_q = nc.dram_tensor("wuT_q", [D, II], BF16, kind=ktap).ap()
    wdT_q = nc.dram_tensor("wdT_q", [II, D], BF16, kind=ktap).ap()
    qS_d = nc.dram_tensor("qS_d", [DC, 128, L], BF16, kind=ktap).ap()
    kS_d = nc.dram_tensor("kS_d", [DC, 128, L], BF16, kind=ktap).ap()
    vS_d = nc.dram_tensor("vS_d", [TT, 128, 16 * 65], BF16, kind=ktap).ap()
    x1_d = nc.dram_tensor("x1_d", [TT, 128, D], F32, kind=ktap).ap()
    xr_d = nc.dram_tensor("xr_d", [TT, 128, D], F32, kind=ktap).ap()
    hb_d = nc.dram_tensor("hb_d", [2, 64, 512], BF16).ap()

    es = ExitStack()
    sbt = lambda name, shape, dt: es.enter_context(nc.sbuf_tensor(name, shape, dt))
    psA = es.enter_context(nc.psum_tensor("psA", [128, 2048], F32))
    ps2a = es.enter_context(nc.psum_tensor("ps2a", [128, 1024], F32))
    ps2b = es.enter_context(nc.psum_tensor("ps2b", [128, 1024], F32))

    slabA = sbt("slabA", [128, 24576], BF16)
    slabB = sbt("slabB", [128, 16384], BF16)
    qp = sbt("qp", [128, 4096], BF16)
    kp = sbt("kp", [128, 4096], BF16)
    vp = sbt("vp", [128, 4160], BF16)
    wgu = sbt("wgu", [128, 4096], BF16)
    wdb = sbt("wdb", [128, 2048], BF16)
    PTb = sbt("PTb", [128, 4096], BF16)
    qtile = sbt("qtile", [128, 2048], BF16)
    qkstage = sbt("qkstage", [128, 4096], BF16)
    hbnc = sbt("hbnc", [128, 1024], BF16)

    qsc = sbt("qsc", [128, 3072], F32)
    n_, a_ = qsc[:, 0:512], qsc[:, 512:1024]
    acA, acB = qsc[:, 1024:1536], qsc[:, 1536:2048]
    sg_, ss_ = qsc[:, 2048:2560], qsc[:, 2560:3072]
    xsqv = qsc[:, 0:1024]
    amx3 = sbt("amx3", [128, 768], F32)

    def qslot_views(j):
        b = (j % 4) * 192
        return (amx3[:, b:b + 64], amx3[:, b + 64:b + 128],
                amx3[:, b + 128:b + 192])
    pool = sbt("pool", [128, 7168], F32)
    stat = sbt("stat", [128, 512], F32)
    st_sx = stat[:, 0:64]
    st_sq = stat[:, 64:128]
    st_mean = stat[:, 128:192]
    st_ex2 = stat[:, 192:256]
    st_m2 = stat[:, 256:320]
    st_var = stat[:, 320:384]
    st_va = stat[:, 384:448]
    st_rs = stat[:, 448:512]
    dmy = sbt("dmy", [128, 1024], F32)
    strip = sbt("strip", [128, 1024], F32)
    rbr = sbt("rbr", [128, 1024], F32)
    hh = strip[:, 0:512]
    sgb = rbr[:, 0:512]
    ones66 = sbt("ones66", [128, 66], F32)
    boS = sbt("boS", [128, 1024], F32)
    bvS = sbt("bvS", [128, 1024], F32)
    bqkvS = sbt("bqkvS", [128, 16], F32)

    xb = [pool[:, 0:1024], pool[:, 1024:2048]]
    x1b = [pool[:, 2048:3072], pool[:, 3072:4096]]
    ob = [pool[:, 4096:5120], pool[:, 5120:6144]]
    xt = pool[:, 6144:7168]
    woTS = slabA[:, 0:8192]

    sP = Sem(es.enter_context(nc.semaphore("sP")))
    sA = Sem(es.enter_context(nc.semaphore("sA")))
    sV = Sem(es.enter_context(nc.semaphore("sV")))
    names = ["sC", "sWF0", "sWF1", "sWF2", "sWF3", "sWST", "sX0", "sX1",
             "sOB0", "sOB1",
             "sQK", "sQK2", "sVS0", "sVS1", "sPQ0", "sPQ1", "sPK0", "sPK1",
             "sPV0", "sPV1", "sHB0", "sHB1", "sHBL0", "sHBL1",
             "sX10", "sX11", "sWGU", "sWD0", "sWD1",
             "sTQ0", "sTQ1", "sTH0", "sTH1", "sWKV", "sCC"]
    S = {nm: Sem(es.enter_context(nc.semaphore(nm))) for nm in names}
    sC = S["sC"]
    sWF = [S["sWF0"], S["sWF1"], S["sWF2"], S["sWF3"]]
    sCC = S["sCC"]
    sWST = S["sWST"]
    sX = [S["sX0"], S["sX1"]]
    sOB = [S["sOB0"], S["sOB1"]]
    sQK = S["sQK"]
    sQK2 = S["sQK2"]
    sVS = [S["sVS0"], S["sVS1"]]
    sPQ = [S["sPQ0"], S["sPQ1"]]
    sPK = [S["sPK0"], S["sPK1"]]
    sPV = [S["sPV0"], S["sPV1"]]
    sHB = [S["sHB0"], S["sHB1"]]
    sHBL = [S["sHBL0"], S["sHBL1"]]
    sX1 = [S["sX10"], S["sX11"]]
    sWGU = S["sWGU"]
    sWD = [S["sWD0"], S["sWD1"]]
    sTQ = [S["sTQ0"], S["sTQ1"]]
    sTH = [S["sTH0"], S["sTH1"]]
    sWKV = S["sWKV"]

    ops = {"t": [], "v": [], "s": [], "y": [], "g": []}

    def op(eng, fn, wait=(), inc=()):
        wait = [(s, v) for (s, v) in wait if v > 0]
        for s, v in wait:
            assert v <= s.n, f"wait on future value {v} > {s.n}"
        for s, k in inc:
            s.n += k
        vals = [(s.h, v) for (s, v) in wait]
        incs = [(s.h, k) for (s, k) in inc]

        def run(e):
            for h, v in vals:
                e.wait_ge(h, v)
            inst = fn(e)
            for h, k in incs:
                inst.then_inc(h, k)
        ops[eng].append(run)

    def dma(eng, sem, out, in_, wait=(), transpose=False):
        if transpose:
            op(eng, lambda e: e.dma_start_transpose(out, in_), wait=wait,
               inc=[(sem, 16)])
        else:
            op(eng, lambda e: e.dma_start(out=out, in_=in_), wait=wait,
               inc=[(sem, 16)])

    dma("y", sC, bqkvS[:, :], bqkv_d[:, :])
    dma("y", sC, bvS[:, :], bv_d.broadcast_to([128, D]))
    dma("y", sC, boS[:, :], bo_d.broadcast_to([128, D]))
    const_c = sC.n
    op("v", lambda e: e.memset(ones66[:, :], 1.0))
    op("v", lambda e: e.memset(amx3[:, :], 1.0))
    op("v", lambda e: e.memset(stat[:, :], 1.0))
    op("v", lambda e: e.memset(dmy[:, :], 0.0), inc=[(sV, 1)])

    qrec = []
    qctr = [0]
    slot_guard = {0: [], 1: [], 2: [], 3: []}

    def q_emit_ts(r):
        a, sc, _ = r["sv"]
        op("v", lambda e, sc=sc, a=a: e.tensor_scalar(
            sc[:, :], a[:, :], 1.0 / 6.0, 1e-12, Alu.mult, Alu.max))

    def q_emit_recip(r):
        _, sc, rr = r["sv"]
        op("v", lambda e, rr=rr, sc=sc: e.reciprocal(rr[:, :], sc[:, :]))

    def q_emit_apply(r):
        w, nb = r["w"], r["w"] // 16
        _, sc, rr = r["sv"]
        src3 = r["src"].rearrange("p (b k) -> p b k", k=16)
        rc3 = rr[:, 0:nb].unsqueeze(2).broadcast_to([128, nb, 16])
        sc3 = sc[:, 0:nb].unsqueeze(2).broadcast_to([128, nb, 16])
        nv = n_[:, 0:w]
        n3 = nv.rearrange("p (b k) -> p b k", k=16)
        op("v", lambda e, n3=n3, src3=src3, rc3=rc3: e.tensor_tensor(
            n3, src3, rc3, Alu.mult),
            wait=list(slot_guard[r["slot"]]), inc=[(sV, 1)])
        nready = sV.n
        op("s", lambda e, w=w, nv=nv: e.activation(sg_[:, 0:w], nv, Act.Sign),
           wait=[(sV, nready)], inc=[(sA, 1)])
        sgnready = sA.n
        av = a_[:, 0:w]
        op("s", lambda e, av=av, nv=nv: e.activation(av, nv, Act.Abs),
           inc=[(sA, 1)])
        absready = sA.n
        op("v", lambda e, av=av, w=w: e.tensor_scalar(
            acA[:, 0:w], av, 0.25, None, Alu.is_gt), wait=[(sA, absready)])
        for t in (0.75, 1.25, 1.75):
            op("v", lambda e, t=t, av=av, w=w: e.scalar_tensor_tensor(
                acA[:, 0:w], av, t, acA[:, 0:w], Alu.is_gt, Alu.add))
        op("v", lambda e, av=av, w=w: e.tensor_scalar(
            acB[:, 0:w], av, 5.0, 2.0, Alu.is_gt, Alu.mult))
        for t in (2.5, 3.5):
            op("v", lambda e, t=t, av=av, w=w: e.scalar_tensor_tensor(
                acB[:, 0:w], av, t, acB[:, 0:w], Alu.is_gt, Alu.add))
        op("v", lambda e, w=w: e.scalar_tensor_tensor(
            acA[:, 0:w], acA[:, 0:w], 0.5, acB[:, 0:w], Alu.mult, Alu.add))
        s3 = ss_[:, 0:w].rearrange("p (b k) -> p b k", k=16)
        sg3 = sg_[:, 0:w].rearrange("p (b k) -> p b k", k=16)
        op("v", lambda e, s3=s3, sg3=sg3, sc3=sc3: e.tensor_tensor(
            s3, sg3, sc3, Alu.mult), wait=[(sA, sgnready)])
        op("v", lambda e, dst=r["dst"], w=w: e.tensor_tensor(
            dst[:, 0:w], acA[:, 0:w], ss_[:, 0:w], Alu.mult), inc=[(sV, 1)])
        vready = sV.n
        flush_tr([(sV, vready)])
        tr = r["tr"]
        for j, (sem, dsl, extra) in enumerate(tr):
            pend_tr.append((sem, dsl, r["dst"][:, j * 128:(j + 1) * 128],
                            extra))
        if tr:
            slot_guard[r["slot"]] = [(tr[0][0], tr[0][0].n + len(tr) * 16)]
        if r["post"] is not None:
            r["post"](vready)

    def q_push(src, w, tr=(), wait_red=(), post=None):
        slot = qctr[0] % 4
        qctr[0] += 1
        sv = qslot_views(slot)
        nb = w // 16
        src3 = src.rearrange("p (b k) -> p b k", k=16)
        a = sv[0]
        op("v", lambda e, a=a, src3=src3, nb=nb: e.tensor_reduce(
            a[:, 0:nb], src3, AX, Alu.max, apply_absolute_value=True),
            wait=wait_red)
        qrec.append({"w": w, "sv": sv, "slot": slot, "src": src,
                     "dst": qtile[:, slot * 512: slot * 512 + 512],
                     "tr": list(tr), "post": post})
        if len(qrec) >= 2:
            q_emit_ts(qrec[-2])
        if len(qrec) >= 3:
            q_emit_recip(qrec[-3])
        if len(qrec) >= 4:
            q_emit_apply(qrec.pop(0))

    def q_flush():
        assert len(qrec) == 3, f"q_flush with {len(qrec)} pending"
        for _ in range(3):
            q_push(dmy[:, 0:256], 256)
        c0, c1, c2 = qrec
        q_emit_apply(c0)
        q_emit_ts(c2)
        q_emit_recip(c1)
        q_emit_apply(c1)
        q_emit_recip(c2)
        q_emit_apply(c2)
        qrec.clear()

    def ln_chain(srcb, dst, wait_first, wait_norm):
        src8 = srcb.rearrange("p (c w) -> p c w", w=128)
        op("s", lambda e: e.activation(xsqv, srcb, Act.Square),
           wait=list(wait_first) + [(sV, sV.n)], inc=[(sA, 1)])
        sqd = sA.n
        op("v", lambda e: e.tensor_reduce(st_sx[:, 0:1], srcb, AX, Alu.add),
           wait=wait_first)
        op("v", lambda e: e.tensor_reduce(st_sq[:, 0:1], xsqv, AX, Alu.add),
           wait=[(sA, sqd)])
        for k in (1, 2, 4):
            op("v", lambda e, k=k: e.tensor_copy(st_sx[:, k:2 * k], st_sx[:, 0:k]))
            op("v", lambda e, k=k: e.tensor_copy(st_sq[:, k:2 * k], st_sq[:, 0:k]))
        op("v", lambda e: e.tensor_scalar(st_mean, st_sx, 1.0 / D, None, Alu.mult))
        op("v", lambda e: e.tensor_scalar(st_ex2, st_sq, 1.0 / D, None, Alu.mult))
        op("v", lambda e: e.tensor_tensor(st_m2, st_mean, st_mean, Alu.mult))
        op("v", lambda e: e.tensor_tensor(st_var, st_ex2, st_m2, Alu.subtract))
        op("v", lambda e: e.tensor_scalar(st_va, st_var, EPS, None, Alu.add),
           inc=[(sV, 1)])
        vv = sV.n
        op("s", lambda e: e.activation(st_rs, st_va, Act.Ln), wait=[(sV, vv)])
        op("s", lambda e: e.activation(st_rs, st_rs, Act.Exp, scale=-0.5),
           inc=[(sA, 1)])
        aa = sA.n
        op("v", lambda e: e.engine_nop(), wait=[(sA, aa)] + wait_norm)
        mb = st_mean[:, 0:8].unsqueeze(2).broadcast_to([128, 8, 128])
        rb = st_rs[:, 0:8].unsqueeze(2).broadcast_to([128, 8, 128])
        xc8 = xsqv.rearrange("p (c w) -> p c w", w=128)
        op("v", lambda e: e.tensor_tensor(xc8, src8, mb, Alu.subtract))
        op("v", lambda e: e.tensor_tensor(
            dst.rearrange("p (c w) -> p c w", w=128), xc8, rb, Alu.mult),
            inc=[(sV, 1)])
        return sV.n

    pend_tr = []

    def flush_tr(wait_v):
        for i, (sem, dst, srcap, extra) in enumerate(pend_tr):
            dma("y", sem, dst, srcap,
                wait=(list(wait_v) + extra) if i == 0 else [], transpose=True)
        pend_tr.clear()

    def drain_dummy():
        op("v", lambda e: e.tensor_copy(dmy[:, 0:512], dmy[:, 512:1024]),
           inc=[(sV, 1)])
        return sV.n

    wq_buf_free = [0, 0, 0, 0]
    wfs = [pool[:, 0:128], pool[:, 128:256], pool[:, 256:384],
           pool[:, 384:512]]
    dma("y", sWF[0], pool[:, 3072:5888], wds_d[:, :])
    wd_dv = sWF[0].n
    wti = 0
    for mi, (src_d, strip0) in enumerate([(wgs_d, 0), (wus_d, 2816)]):
        for it in range(ICN):
            sl = wti % 4
            wti += 1
            dma("y", sWF[sl], wfs[sl][:, :], src_d[it * 128:(it + 1) * 128, :],
                wait=[(sV, wq_buf_free[sl])])
            dv = sWF[sl].n
            tr = [(sTQ[qctr[0] % 2],
                   slabA[:, strip0 + it * 128: strip0 + it * 128 + 128], [])]

            def post(vready, sl=sl):
                wq_buf_free[sl] = sV.n
            q_push(wfs[sl][:, :], 128, tr=tr, wait_red=[(sWF[sl], dv)],
                   post=post)
    for ck in range(6):
        c0 = ck * 512
        w = min(512, II - c0)
        tr = []
        for j in range(w // 128):
            c = (c0 + j * 128) // 128
            tr.append((sTQ[qctr[0] % 2],
                       slabA[:, 5632 + c * 128: 5632 + c * 128 + 128], []))
        q_push(pool[:, 3072 + c0:3072 + c0 + w], w, tr=tr,
               wait_red=[(sWF[0], wd_dv)])
    q_flush()
    tqw = [(sTQ[0], sTQ[0].n), (sTQ[1], sTQ[1].n)]
    dma("y", sWST, wq_st[0], slabA[:, 0:2816], wait=tqw)
    dma("y", sWST, wq_st[1], slabA[:, 2816:5632])
    dma("y", sWST, wq_st[2], slabA[:, 5632:8448])
    w_done = sWST.n
    w_quant_v = sV.n

    def emit_collective():
        op("g", lambda e: e.collective_compute(
            "AllGather", Alu.bypass, [list(range(8))],
            ins=[wq_st[:, :, :]],
            outs=[G_all[:, :, :, :]]),
            wait=[(sWST, w_done)], inc=[(sCC, 1)])
        cc = sCC.n
        if os.environ.get("KTAP") == "1":
            dma("y", sC, wgT_q.rearrange("(r p) w -> r p w", r=8),
                G_all[:, 0, :, :], wait=[(sCC, cc)])
            dma("y", sC, wuT_q.rearrange("(r p) w -> r p w", r=8),
                G_all[:, 1, :, :])
            for rr in range(8):
                dma("y", sC,
                    wdT_q[:, rr * 128:(rr + 1) * 128]
                    .rearrange("(c q) d -> c q d", q=128),
                    G_all[rr, 2, :, :].rearrange("q (c d) -> c q d", d=128))
        return cc

    xb_free = [0, 0]
    x1b_free = [0, 0]
    ob_free = [0, 0]
    qk_free = [0, 0]
    hb_free = [0, 0]
    wgu_free = 0
    wdb_free = [0, 0]
    qp_free = [0, 0]
    kp_free = [0, 0]
    vp_free = [0, 0]
    PT_free = [0, 0]
    vst_free = [0, 0]
    xrst = [None] * TT
    x1st = [None] * TT

    def load_wqkv(wait0):
        for dc in range(DC):
            dma("g", sWKV, slabA[:, dc * 3072:(dc + 1) * 3072],
                wqkvT_d[dc * 128:(dc + 1) * 128, :],
                wait=(wait0 if dc == 0 else []))
        return sWKV.n

    wkv_ready = load_wqkv([(sWST, w_done)])
    cc_done = emit_collective()
    exp_done = 0
    down_evac_v = 0

    for s in range(NSEQ):
        seq_start_p = sP.n

        for tt in range(TT):
            sl = tt % 2
            dma("y", sX[sl], xb[sl][:, :], x_d[s, tt * 128:(tt + 1) * 128, :],
                wait=[(sV, xb_free[sl])] +
                     ([(sV, w_quant_v), (sWST, w_done)]
                      if (s == 0 and tt == 0) else []))
            dx = sX[sl].n
            qsl = tt % 2
            qt_ = qtile[:, qsl * 1024:(qsl + 1) * 1024]
            vq = ln_chain(xb[sl][:, :], qt_,
                          wait_first=[(sX[sl], dx)],
                          wait_norm=slot_guard[2 * qsl] +
                                    slot_guard[2 * qsl + 1])
            op("v", lambda e, o=ob[sl], b=xb[sl], q=qt_: e.tensor_tensor(
                o, b[:, :], q, Alu.add),
                wait=[(sOB[sl], ob_free[sl])], inc=[(sV, 1)])
            xb_free[sl] = sV.n
            dma("y", sOB[sl], xr_d[tt], ob[sl], wait=[(sV, sV.n)])
            ob_free[sl] = sOB[sl].n
            xrst[tt] = (sl, sOB[sl].n)
            flush_tr([(sV, vq)])
            for c in range(DC):
                extra = [(sP, seq_start_p)] if (tt == 0 and c == 0) else []
                pend_tr.append((sTQ[qsl],
                                slabB[:, c * 2048 + tt * 128:
                                      c * 2048 + tt * 128 + 128],
                                qt_[:, c * 128:(c + 1) * 128], extra))
            slot_guard[2 * qsl] = [(sTQ[qsl], sTQ[qsl].n + 8 * 16)]
            slot_guard[2 * qsl + 1] = [(sTQ[qsl], sTQ[qsl].n + 8 * 16)]
        flush_tr([(sV, drain_dummy())])
        xnT0, xnT1 = sTQ[0].n, sTQ[1].n

        evac_prev = [0, 0, 0, 0]
        for ec in range(2 * DC):
            ssl = ec % 2
            stg = qkstage[:, ssl * 2048:(ssl + 1) * 2048]
            sQKs = [sQK, sQK2][ssl]
            for qt in range(4):
                for dc in range(DC):
                    w0 = []
                    if dc == 0:
                        w0 = [(sA, evac_prev[qt])]
                        if qt == 0:
                            w0 += [(sV, sV.n)]
                        if ec == 0 and qt == 0:
                            w0 += [(sWKV, wkv_ready), (sTQ[0], xnT0),
                                   (sTQ[1], xnT1)]
                    op("t", lambda e, ec=ec, dc=dc, qt=qt: e.matmul(
                        psA[:, qt * 512:(qt + 1) * 512],
                        slabA[:, dc * 3072 + ec * 128: dc * 3072 + ec * 128 + 128],
                        slabB[:, dc * 2048 + qt * 512: dc * 2048 + qt * 512 + 512],
                        start=(dc == 0), stop=(dc == DC - 1)),
                        wait=w0,
                        inc=([(sP, 1)] if dc == DC - 1 else []))
                mmv = sP.n
                op("s", lambda e, ec=ec, qt=qt, stg=stg: e.activation(
                    stg[:, qt * 512:(qt + 1) * 512],
                    psA[:, qt * 512:(qt + 1) * 512],
                    Act.Identity, bias=bqkvS[:, ec:ec + 1]),
                    wait=[(sP, mmv)] +
                         ([(sQKs, qk_free[ssl])] if qt == 0 else []) +
                         ([(sC, const_c)] if (s == 0 and ec == 0 and qt == 0)
                          else []),
                    inc=[(sA, 1)])
                evac_prev[qt] = sA.n
            ev = sA.n
            dst = (qS_d if ec < DC else kS_d)[ec % DC]
            dma("y", sQKs, dst[:, :], stg, wait=[(sA, ev)])
            qk_free[ssl] = sQKs.n

        for tt in range(TT):
            pse = [ps2a, ps2b][tt % 2]
            for dc in range(DC):
                for nh2 in range(2):
                    w0 = []
                    if dc == 0 and nh2 == 0:
                        w0 = [(sV, sV.n)]
                    op("t", lambda e, tt=tt, dc=dc, nh2=nh2, p=pse: e.matmul(
                        p[:, nh2 * 512:(nh2 + 1) * 512],
                        slabB[:, dc * 2048 + tt * 128: dc * 2048 + tt * 128 + 128],
                        slabA[:, dc * 3072 + 2048 + nh2 * 512:
                              dc * 3072 + 2048 + nh2 * 512 + 512],
                        start=(dc == 0), stop=(dc == DC - 1)),
                        wait=w0,
                        inc=([(sP, 1)] if (dc == DC - 1 and nh2 == 1) else []))
            mmv = sP.n
            vsl = tt % 2
            vstage = PTb[:, vsl * 2048:vsl * 2048 + 1040]
            v3 = vstage.rearrange("p (h c) -> p h c", c=65)
            op("v", lambda e, v3=v3: e.memset(v3[:, :, 64:65], 1.0),
               wait=[(sP, mmv), (sVS[vsl], vst_free[vsl])] +
                    ([(sC, const_c)] if (s == 0 and tt == 0) else []))
            op("v", lambda e, v3=v3, p=pse: e.tensor_tensor(
                v3[:, :, 0:64], p[:, :].rearrange("p (h c) -> p h c", c=64),
                bvS[:, :].rearrange("p (h c) -> p h c", c=64), Alu.add),
                inc=[(sV, 1)])
            dma("y", sVS[vsl], vS_d[tt], vstage, wait=[(sV, sV.n)])
            vst_free[vsl] = sVS[vsl].n
        stage_done = [(sQK, sQK.n), (sQK2, sQK2.n),
                      (sVS[0], sVS[0].n), (sVS[1], sVS[1].n)]
        qkv_done_p = sP.n
        qkv_done_v = sV.n

        av_guard_v = qkv_done_v
        rep_guard_v = qkv_done_v
        rep_guard_p = sP.n
        if exp_done == 0:
            exp_done = sA.n
        hb_slot = 0
        first_exp = True
        for p in range(DC):
            slq = p % 2
            dma("y", sPQ[slq], qp[:, slq * 2048:(slq + 1) * 2048], qS_d[p],
                wait=[(sP, qp_free[slq])] + (stage_done if p < 2 else []))
            dma("y", sPK[slq], kp[:, slq * 2048:(slq + 1) * 2048], kS_d[p],
                wait=[(sP, kp_free[slq])])
            for g4 in range(4):
                dma("y", sPV[slq],
                    vp[:, slq * 2080 + g4 * 520: slq * 2080 + (g4 + 1) * 520]
                    .rearrange("p (t c) -> p t c", c=130),
                    vS_d[g4 * 4:(g4 + 1) * 4, :, p * 130:(p + 1) * 130]
                    .transpose([1, 0, 2]),
                    wait=[(sP, vp_free[slq])] if g4 == 0 else [])
            pair_ld = [(sPQ[slq], sPQ[slq].n), (sPK[slq], sPK[slq].n),
                       (sPV[slq], sPV[slq].n)]
            qpb = qp[:, slq * 2048:(slq + 1) * 2048]
            kpb = kp[:, slq * 2048:(slq + 1) * 2048]
            vpb = vp[:, slq * 2080:(slq + 1) * 2080].rearrange(
                "p (t c) -> p t c", c=130)
            for qc in range(4):
                for ktp in range(8):
                    ptsl = ktp % 2
                    PT = PTb[:, ptsl * 2048:(ptsl + 1) * 2048]
                    for kt2 in range(2):
                        kt = ktp * 2 + kt2
                        w0 = []
                        if kt2 == 0:
                            w0 = [(sA, exp_done)]
                            if qc == 0 and ktp == 0:
                                w0 += pair_ld
                        op("t", lambda e, kt=kt, kt2=kt2, q=qpb, k=kpb, qc=qc:
                            e.matmul(
                                psA[:, kt2 * 512:(kt2 + 1) * 512],
                                k[0:64, kt * 128:(kt + 1) * 128],
                                q[0:64, qc * 512:(qc + 1) * 512],
                                start=True, stop=True, tile_position=(0, 0)),
                            wait=w0)
                        op("t", lambda e, kt=kt, kt2=kt2, q=qpb, k=kpb, qc=qc:
                            e.matmul(
                                psA[:, 1024 + kt2 * 512: 1024 + (kt2 + 1) * 512],
                                k[64:128, kt * 128:(kt + 1) * 128],
                                q[64:128, qc * 512:(qc + 1) * 512],
                                start=True, stop=True, tile_position=(64, 0)),
                            inc=([(sP, 1)] if kt2 == 1 else []))
                    sc_done = sP.n
                    ew = [(sP, sc_done), (sP, PT_free[ptsl])]
                    if first_exp:
                        ew += stage_done
                        if ktp == 1:
                            first_exp = False
                    op("s", lambda e, PT=PT: e.activation(PT, psA[:, :], Act.Exp,
                                                          scale=SCALE),
                       wait=ew, inc=[(sA, 1)])
                    exp_done = sA.n
                    for kt2 in range(2):
                        kt = ktp * 2 + kt2
                        w0 = []
                        if kt2 == 0:
                            w0 = [(sA, exp_done)]
                            if ktp == 0:
                                w0 += [(sV, av_guard_v)]
                        op("t", lambda e, kt=kt, kt2=kt2, ktp=ktp, v=vpb, PT=PT:
                            e.matmul(
                                ps2a[0:65, 0:512], v[:, kt, 0:65],
                                PT[:, kt2 * 512:(kt2 + 1) * 512],
                                start=(ktp == 0 and kt2 == 0),
                                stop=(ktp == 7 and kt2 == 1)),
                            wait=w0)
                        op("t", lambda e, kt=kt, kt2=kt2, ktp=ktp, v=vpb, PT=PT:
                            e.matmul(
                                ps2a[0:65, 512:1024], v[:, kt, 65:130],
                                PT[:, 1024 + kt2 * 512: 1024 + (kt2 + 1) * 512],
                                start=(ktp == 0 and kt2 == 0),
                                stop=(ktp == 7 and kt2 == 1)),
                            inc=([(sP, 1)] if kt2 == 1 else []))
                    PT_free[ptsl] = sP.n
                av_done = sP.n
                op("s", lambda e: e.activation(strip[64:65, 0:1024],
                                               ps2a[64:65, :], Act.Ln),
                   wait=[(sP, av_done), (sP, rep_guard_p)])
                op("s", lambda e: e.activation(strip[64:65, 0:1024],
                                               strip[64:65, 0:1024], Act.Exp,
                                               scale=-1.0), inc=[(sA, 1)])
                rec_done = sA.n
                for nh2 in range(2):
                    op("t", lambda e, nh2=nh2: e.matmul(
                        ps2b[0:65, nh2 * 512:(nh2 + 1) * 512],
                        ones66[64:65, 0:65],
                        strip[64:65, nh2 * 512:(nh2 + 1) * 512],
                        start=True, stop=True, tile_position=(64, 0)),
                        wait=([(sV, rep_guard_v), (sA, rec_done)]
                              if nh2 == 0 else []),
                        inc=([(sP, 1)] if nh2 == 1 else []))
                rep_guard_p = sP.n
                op("v", lambda e: e.tensor_copy(rbr[0:64, :], ps2b[0:64, :]),
                   wait=[(sP, rep_guard_p)], inc=[(sV, 1)])
                rep_guard_v = sV.n
                op("v", lambda e, p=p, qc=qc: e.tensor_tensor(
                    slabB[0:64, p * 2048 + qc * 512: p * 2048 + qc * 512 + 512],
                    ps2a[0:64, 0:512], rbr[0:64, 0:512], Alu.mult))
                hsl = hb_slot % 2
                op("v", lambda e, hsl=hsl: e.tensor_tensor(
                    hbnc[0:64, hsl * 512:(hsl + 1) * 512],
                    ps2a[0:64, 512:1024], rbr[0:64, 512:1024], Alu.mult),
                    wait=[(sHBL[hsl], hb_free[hsl])], inc=[(sV, 1)])
                av_guard_v = sV.n
                dma("y", sHB[hsl], hb_d[hsl], hbnc[0:64, hsl * 512:(hsl + 1) * 512],
                    wait=[(sV, av_guard_v)])
                wb = sHB[hsl].n
                dma("g", sHBL[hsl],
                    slabB[64:128, p * 2048 + qc * 512: p * 2048 + qc * 512 + 512],
                    hb_d[hsl], wait=[(sHB[hsl], wb)])
                hb_free[hsl] = sHBL[hsl].n
                hb_slot += 1
            qp_free[slq] = sP.n
            kp_free[slq] = sP.n
            vp_free[slq] = sP.n
        attn_done_hbl = [(sHBL[0], sHBL[0].n), (sHBL[1], sHBL[1].n)]
        attn_done_v = sV.n

        dma("g", sWKV, woTS.rearrange("p (c d2) -> p c d2", d2=1024),
            woT_d.rearrange("(c p) d2 -> p c d2", p=128),
            wait=[(sP, qkv_done_p)])
        woTS_g = sWKV.n
        oproj_evac_v = 0
        for tt in range(TT):
            pse = [ps2a, ps2b][tt % 2]
            for dc in range(DC):
                for nh2 in range(2):
                    w0 = []
                    if dc == 0 and nh2 == 0:
                        w0 = [(sV, attn_done_v if tt < 2 else oproj_evac_v)]
                        if tt == 0:
                            w0 += attn_done_hbl + [(sWKV, woTS_g),
                                                   (sV, attn_done_v)]
                    op("t", lambda e, tt=tt, dc=dc, nh2=nh2, p=pse: e.matmul(
                        p[:, nh2 * 512:(nh2 + 1) * 512],
                        slabB[:, dc * 2048 + tt * 128: dc * 2048 + tt * 128 + 128],
                        woTS[:, dc * 1024 + nh2 * 512: dc * 1024 + nh2 * 512 + 512],
                        start=(dc == 0), stop=(dc == DC - 1)),
                        wait=w0,
                        inc=([(sP, 1)] if (dc == DC - 1 and nh2 == 1) else []))
            mmv = sP.n
            sl = tt % 2
            dma("y", sX[sl], xb[sl][:, :], xr_d[tt],
                wait=[(sV, xb_free[sl]), (sOB[xrst[tt][0]], xrst[tt][1])])
            dx = sX[sl].n
            op("v", lambda e, o=ob[sl], p2=pse: e.tensor_tensor(
                o, p2[:, :], boS[:, :], Alu.add),
                wait=[(sP, mmv), (sX[sl], dx), (sOB[sl], ob_free[sl])] +
                     ([(sC, const_c)] if (s == 0 and tt == 0) else []))
            op("v", lambda e, o=ob[sl], b=xb[sl]: e.tensor_tensor(
                o, o, b[:, :], Alu.add), inc=[(sV, 1)])
            xb_free[sl] = sV.n
            oproj_evac_v = sV.n
            dma("y", sOB[sl], x1_d[tt], ob[sl], wait=[(sV, oproj_evac_v)])
            ob_free[sl] = sOB[sl].n
            x1st[tt] = (sl, sOB[sl].n)
        oproj_done_p = sP.n
        oproj_done_v = sV.n

        ln_slots = [xt, ob[0], ob[1]]
        for tt in range(TT):
            sl = tt % 2
            dma("y", sX1[sl], x1b[sl][:, :], x1_d[tt],
                wait=[(sV, x1b_free[sl]), (sOB[x1st[tt][0]], x1st[tt][1])])
            dx = sX1[sl].n
            lsl = tt % 3
            wn = [(sOB[lsl - 1], ob_free[lsl - 1])] if (tt in (1, 2)) else []
            ln_chain(x1b[sl][:, :], ln_slots[lsl][:, :],
                     wait_first=[(sX1[sl], dx)], wait_norm=wn)
            x1b_free[sl] = sV.n
            for ck in range(2):
                tr = []
                for j in range(4):
                    c = ck * 4 + j
                    extra = ([(sP, oproj_done_p)]
                             if (tt == 0 and ck == 0 and j == 0) else [])
                    tr.append((sTQ[qctr[0] % 2],
                               slabB[:, c * 2048 + tt * 128:
                                     c * 2048 + tt * 128 + 128], extra))
                q_push(ln_slots[lsl][:, ck * 512:(ck + 1) * 512], 512, tr=tr)
        q_flush()
        xqT = [(sTQ[0], sTQ[0].n), (sTQ[1], sTQ[1].n)]

        gu_evac_v = oproj_done_v
        silu_done = sA.n
        for tg in range(2):
            tg_start_p = sP.n
            for icb in range(ICB):
                dma("g", sWGU, wgu[:, 0:2048].rearrange("p (c w) -> p c w", w=256),
                    G_all[:, 0, :, icb * 256:(icb + 1) * 256]
                    .rearrange("c p w -> p c w"),
                    wait=[(sP, wgu_free)] +
                         ([(sCC, cc_done)] + xqT if (tg == 0 and icb < 1)
                          else []))
                dma("g", sWGU, wgu[:, 2048:4096].rearrange("p (c w) -> p c w", w=256),
                    G_all[:, 1, :, icb * 256:(icb + 1) * 256]
                    .rearrange("c p w -> p c w"))
                wld = sWGU.n
                for tt8 in range(8):
                    tt = tg * 8 + tt8
                    gb = (tt8 % 2) * 1024
                    for dc in range(DC):
                        w0 = []
                        if dc == 0:
                            w0 = [(sWGU, wld), (sV, gu_evac_v), (sA, silu_done),
                                  (sV, down_evac_v)]
                            if tg == 0 and icb == 0 and tt8 == 0:
                                w0 += [(sP, qkv_done_p), (sA, exp_done)]
                        op("t", lambda e, tt=tt, dc=dc, gb=gb: e.matmul(
                            psA[:, gb:gb + 256],
                            slabB[:, dc * 2048 + tt * 128:
                                  dc * 2048 + tt * 128 + 128],
                            wgu[:, dc * 256:(dc + 1) * 256],
                            start=(dc == 0), stop=(dc == DC - 1)), wait=w0)
                        op("t", lambda e, tt=tt, dc=dc, gb=gb: e.matmul(
                            psA[:, gb + 512:gb + 768],
                            slabB[:, dc * 2048 + tt * 128:
                                  dc * 2048 + tt * 128 + 128],
                            wgu[:, 2048 + dc * 256:2048 + (dc + 1) * 256],
                            start=(dc == 0), stop=(dc == DC - 1)),
                            inc=([(sP, 1)] if dc == DC - 1 else []))
                    guv = sP.n
                    op("s", lambda e, gb=gb: e.activation(
                        sgb[:, 0:256], psA[:, gb:gb + 256], Act.Silu),
                        wait=[(sP, guv)], inc=[(sA, 1)])
                    silu_done = sA.n
                    hsl = qctr[0] % 4
                    hh_ = strip[:, hsl * 256:(hsl + 1) * 256]
                    op("v", lambda e, gb=gb, hh_=hh_: e.tensor_tensor(
                        hh_, sgb[:, 0:256], psA[:, gb + 512:gb + 768],
                        Alu.mult), wait=[(sA, silu_done)], inc=[(sV, 1)])
                    gu_evac_v = sV.n
                    tr = []
                    for j in range(2):
                        c = icb * 2 + j
                        extra = ([(sP, tg_start_p)]
                                 if (icb == 0 and tt8 == 0 and j == 0) else [])
                        tr.append((sTH[qctr[0] % 2],
                                   slabA[:, c * 1024 + tt8 * 128:
                                         c * 1024 + tt8 * 128 + 128], extra))
                    q_push(hh_, 256, tr=tr)
                wgu_free = sP.n
            q_flush()
            hqT = [(sTH[0], sTH[0].n), (sTH[1], sTH[1].n)]
            for sub in range(2):
                psv = [psA[:, 0:1024], psA[:, 1024:2048], ps2a[:, :], ps2b[:, :]]
                for ic in range(ICN):
                    wsl = ic % 2
                    dma("g", sWD[wsl], wdb[:, wsl * 1024:(wsl + 1) * 1024]
                        .rearrange("q (r d) -> q r d", r=8),
                        G_all[:, 2, :, ic * 128:(ic + 1) * 128]
                        .rearrange("r q d -> q r d"),
                        wait=[(sP, wdb_free[wsl])] +
                             (hqT + [(sCC, cc_done)] if ic < 2 else []))
                    wld = sWD[wsl].n
                    for t4 in range(4):
                        tt8 = sub * 4 + t4
                        for nh2 in range(2):
                            w0 = []
                            if t4 == 0 and nh2 == 0:
                                w0 = [(sWD[wsl], wld)]
                                if ic == 0:
                                    w0 += [(sV, gu_evac_v), (sA, silu_done),
                                           (sV, down_evac_v if down_evac_v else
                                            oproj_done_v), (sV, attn_done_v)]
                            op("t", lambda e, ic=ic, tt8=tt8, t4=t4, nh2=nh2,
                                wsl=wsl: e.matmul(
                                    psv[t4][:, nh2 * 512:(nh2 + 1) * 512],
                                    slabA[:, ic * 1024 + tt8 * 128:
                                          ic * 1024 + tt8 * 128 + 128],
                                    wdb[:, wsl * 1024 + nh2 * 512:
                                        wsl * 1024 + nh2 * 512 + 512],
                                    start=(ic == 0), stop=(ic == ICN - 1)),
                                wait=w0,
                                inc=([(sP, 1)] if (t4 == 3 and nh2 == 1)
                                     else []))
                    wdb_free[wsl] = sP.n
                dn = sP.n
                for t4 in range(4):
                    tt = tg * 8 + sub * 4 + t4
                    sl = t4 % 2
                    dma("y", sX1[sl], x1b[sl][:, :], x1_d[tt],
                        wait=[(sV, x1b_free[sl]),
                              (sOB[x1st[tt][0]], x1st[tt][1])])
                    dx = sX1[sl].n
                    op("v", lambda e, o=ob[sl], pv=psv[t4], b=x1b[sl]:
                        e.tensor_tensor(o, pv, b[:, :], Alu.add),
                        wait=[(sP, dn), (sX1[sl], dx), (sOB[sl], ob_free[sl])],
                        inc=[(sV, 1)])
                    x1b_free[sl] = sV.n
                    down_evac_v = sV.n
                    dma("y", sOB[sl], out_d[s, tt * 128:(tt + 1) * 128, :],
                        ob[sl], wait=[(sV, down_evac_v)])
                    ob_free[sl] = sOB[sl].n
        if s == 0:
            wkv_ready = load_wqkv([(sP, sP.n)])

    fin0, fin1, fin_c = sOB[0].n, sOB[1].n, sC.n

    with nc.Block() as block:
        @block.tensor
        def _(t):
            for fn in ops["t"]:
                fn(t)

        @block.vector
        def _(v):
            for fn in ops["v"]:
                fn(v)

        @block.scalar
        def _(s_):
            for fn in ops["s"]:
                fn(s_)

        @block.gpsimd
        def _(g):
            for fn in ops["g"]:
                fn(g)

        @block.sync
        def _(y):
            for fn in ops["y"]:
                fn(y)
            y.wait_ge(sOB[0].h, fin0)
            y.wait_ge(sOB[1].h, fin1)
            y.wait_ge(sC.h, fin_c)

    es.close()
    return nc


def kernel(L0_attn_x, L0_attn_w_qkv, L0_attn_b_qkv, L0_attn_w_o, L0_attn_b_o,
           L0_ffn_w_gate, L0_ffn_w_up, L0_ffn_w_down):
    x = np.ascontiguousarray(np.asarray(L0_attn_x, dtype=np.float32))
    w_qkv = np.asarray(L0_attn_w_qkv, dtype=np.float32)
    b_qkv = np.asarray(L0_attn_b_qkv, dtype=np.float32)
    w_o = np.asarray(L0_attn_w_o, dtype=np.float32)
    b_o = np.asarray(L0_attn_b_o, dtype=np.float32)
    wg = np.ascontiguousarray(np.asarray(L0_ffn_w_gate, dtype=np.float32))
    wu = np.ascontiguousarray(np.asarray(L0_ffn_w_up, dtype=np.float32))
    wd = np.ascontiguousarray(np.asarray(L0_ffn_w_down, dtype=np.float32))

    wqkvT = np.ascontiguousarray(w_qkv.T)
    woT = np.ascontiguousarray(w_o.T)
    bqkv_qk = np.ascontiguousarray(b_qkv[:2048].reshape(16, 128).T)
    bv = np.ascontiguousarray(b_qkv[2048:].reshape(1, 1024))
    bo = np.ascontiguousarray(b_o.reshape(1, 1024))

    nc = build_nc()
    in_maps = []
    for i in range(8):
        in_maps.append({
            "x": np.ascontiguousarray(x[2 * i:2 * i + 2]),
            "wqkvT": wqkvT, "woT": woT, "bqkv": bqkv_qk, "bv": bv, "bo": bo,
            "wgs": np.ascontiguousarray(wg[:, i * 128:(i + 1) * 128]),
            "wus": np.ascontiguousarray(wu[:, i * 128:(i + 1) * 128]),
            "wds": np.ascontiguousarray(wd[i * 128:(i + 1) * 128, :]),
        })
    res = run_bass_kernel_spmd(nc, in_maps, core_ids=list(range(8)))
    return np.concatenate([np.asarray(r["out"]) for r in res.results], axis=0)


# version 08 (final)
# speedup vs baseline: 2.6405x; 1.0544x over previous
import numpy as np
import concourse.bass as bass
import concourse.mybir as mybir
from concourse.bass_utils import run_bass_kernel_spmd
from contextlib import ExitStack

F32 = mybir.dt.float32
BF16 = mybir.dt.bfloat16
Alu = mybir.AluOpType
Act = mybir.ActivationFunctionType
AX = mybir.AxisListType.X

D = 1024
L = 2048
NSEQ = 2
TT = 16
DC = 8
II = 2816
ICN = 22
ICB = 11
SCALE = 0.03125
EPS = 1e-5


class Sem:
    def __init__(self, h):
        self.h = h
        self.n = 0


def build_nc():
    import os
    nc = bass.Bass("TRN2", target_bir_lowering=False,
                   debug=(os.environ.get("KDBG") == "1"),
                   detect_race_conditions=False)

    x_d = nc.dram_tensor("x", [NSEQ, L, D], F32, kind="ExternalInput").ap()
    wqkvT_d = nc.dram_tensor("wqkvT", [D, 3 * D], F32, kind="ExternalInput").ap()
    woT_d = nc.dram_tensor("woT", [D, D], F32, kind="ExternalInput").ap()
    bqkv_d = nc.dram_tensor("bqkv", [128, 16], F32, kind="ExternalInput").ap()
    bv_d = nc.dram_tensor("bv", [1, D], F32, kind="ExternalInput").ap()
    bo_d = nc.dram_tensor("bo", [1, D], F32, kind="ExternalInput").ap()
    wgs_d = nc.dram_tensor("wgs", [II, 128], F32, kind="ExternalInput").ap()
    wus_d = nc.dram_tensor("wus", [II, 128], F32, kind="ExternalInput").ap()
    wds_d = nc.dram_tensor("wds", [128, II], F32, kind="ExternalInput").ap()
    out_d = nc.dram_tensor("out", [NSEQ, L, D], F32, kind="ExternalOutput").ap()
    wq_st = nc.dram_tensor("wq_st", [3, 128, II], BF16, kind="Internal").ap()
    G_all = nc.dram_tensor("G_all", [8, 3, 128, II], BF16, kind="Internal",
                           addr_space="Shared").ap()

    ktap = "ExternalOutput" if os.environ.get("KTAP") == "1" else "Internal"
    wgT_q = nc.dram_tensor("wgT_q", [D, II], BF16, kind=ktap).ap()
    wuT_q = nc.dram_tensor("wuT_q", [D, II], BF16, kind=ktap).ap()
    wdT_q = nc.dram_tensor("wdT_q", [II, D], BF16, kind=ktap).ap()
    qS_d = nc.dram_tensor("qS_d", [DC, 128, L], BF16, kind=ktap).ap()
    kS_d = nc.dram_tensor("kS_d", [DC, 128, L], BF16, kind=ktap).ap()
    vS_d = nc.dram_tensor("vS_d", [TT, 128, 16 * 65], BF16, kind=ktap).ap()
    x1_d = nc.dram_tensor("x1_d", [TT, 128, D], F32, kind=ktap).ap()
    xr_d = nc.dram_tensor("xr_d", [TT, 128, D], F32, kind=ktap).ap()
    hb_d = nc.dram_tensor("hb_d", [2, 64, 512], BF16).ap()

    es = ExitStack()
    sbt = lambda name, shape, dt: es.enter_context(nc.sbuf_tensor(name, shape, dt))
    psA = es.enter_context(nc.psum_tensor("psA", [128, 2048], F32))
    ps2a = es.enter_context(nc.psum_tensor("ps2a", [128, 1024], F32))
    ps2b = es.enter_context(nc.psum_tensor("ps2b", [128, 1024], F32))

    slabA = sbt("slabA", [128, 24576], BF16)
    slabB = sbt("slabB", [128, 16384], BF16)
    qp = sbt("qp", [128, 4096], BF16)
    kp = sbt("kp", [128, 4096], BF16)
    vp = sbt("vp", [128, 4160], BF16)
    wgu = sbt("wgu", [128, 4096], BF16)
    wdb = sbt("wdb", [128, 2048], BF16)
    PTb = sbt("PTb", [128, 4096], BF16)
    qtile = sbt("qtile", [128, 2048], BF16)
    qkstage = sbt("qkstage", [128, 4096], BF16)
    hbnc = sbt("hbnc", [128, 1024], BF16)

    qsc = sbt("qsc", [128, 3072], F32)
    n_, a_ = qsc[:, 0:512], qsc[:, 512:1024]
    acA, acB = qsc[:, 1024:1536], qsc[:, 1536:2048]
    sg_, ss_ = qsc[:, 2048:2560], qsc[:, 2560:3072]
    xsqv = qsc[:, 0:1024]
    amx3 = sbt("amx3", [128, 768], F32)

    def qslot_views(j):
        b = (j % 4) * 192
        return (amx3[:, b:b + 64], amx3[:, b + 64:b + 128],
                amx3[:, b + 128:b + 192])
    pool = sbt("pool", [128, 7168], F32)
    stat = sbt("stat", [128, 512], F32)
    st_sx = stat[:, 0:64]
    st_sq = stat[:, 64:128]
    st_mean = stat[:, 128:192]
    st_ex2 = stat[:, 192:256]
    st_m2 = stat[:, 256:320]
    st_var = stat[:, 320:384]
    st_va = stat[:, 384:448]
    st_rs = stat[:, 448:512]
    dmy = sbt("dmy", [128, 1024], F32)
    strip = sbt("strip", [128, 1024], F32)
    rbr = sbt("rbr", [128, 1024], F32)
    hh = strip[:, 0:512]
    sgb = rbr[:, 0:512]
    ones66 = sbt("ones66", [128, 66], F32)
    boS = sbt("boS", [128, 1024], F32)
    bvS = sbt("bvS", [128, 1024], F32)
    bqkvS = sbt("bqkvS", [128, 16], F32)

    xb = [pool[:, 0:1024], pool[:, 1024:2048]]
    x1b = [pool[:, 2048:3072], pool[:, 3072:4096]]
    ob = [pool[:, 4096:5120], pool[:, 5120:6144]]
    xt = pool[:, 6144:7168]
    woTS = slabA[:, 0:8192]

    sP = Sem(es.enter_context(nc.semaphore("sP")))
    sA = Sem(es.enter_context(nc.semaphore("sA")))
    sV = Sem(es.enter_context(nc.semaphore("sV")))
    names = ["sC", "sWF0", "sWF1", "sWF2", "sWF3", "sWST", "sX0", "sX1",
             "sOB0", "sOB1",
             "sQK", "sQK2", "sVS0", "sVS1", "sPQ0", "sPQ1", "sPK0", "sPK1",
             "sPV0", "sPV1", "sHB0", "sHB1", "sHBL0", "sHBL1",
             "sX10", "sX11", "sWGU", "sWD0", "sWD1",
             "sTQ0", "sTQ1", "sTH0", "sTH1", "sWKV", "sCC"]
    S = {nm: Sem(es.enter_context(nc.semaphore(nm))) for nm in names}
    sC = S["sC"]
    sWF = [S["sWF0"], S["sWF1"], S["sWF2"], S["sWF3"]]
    sCC = S["sCC"]
    sWST = S["sWST"]
    sX = [S["sX0"], S["sX1"]]
    sOB = [S["sOB0"], S["sOB1"]]
    sQK = S["sQK"]
    sQK2 = S["sQK2"]
    sVS = [S["sVS0"], S["sVS1"]]
    sPQ = [S["sPQ0"], S["sPQ1"]]
    sPK = [S["sPK0"], S["sPK1"]]
    sPV = [S["sPV0"], S["sPV1"]]
    sHB = [S["sHB0"], S["sHB1"]]
    sHBL = [S["sHBL0"], S["sHBL1"]]
    sX1 = [S["sX10"], S["sX11"]]
    sWGU = S["sWGU"]
    sWD = [S["sWD0"], S["sWD1"]]
    sTQ = [S["sTQ0"], S["sTQ1"]]
    sTH = [S["sTH0"], S["sTH1"]]
    sWKV = S["sWKV"]

    ops = {"t": [], "v": [], "s": [], "y": [], "g": []}

    def op(eng, fn, wait=(), inc=()):
        wait = [(s, v) for (s, v) in wait if v > 0]
        for s, v in wait:
            assert v <= s.n, f"wait on future value {v} > {s.n}"
        for s, k in inc:
            s.n += k
        vals = [(s.h, v) for (s, v) in wait]
        incs = [(s.h, k) for (s, k) in inc]

        def run(e):
            for h, v in vals:
                e.wait_ge(h, v)
            inst = fn(e)
            for h, k in incs:
                inst.then_inc(h, k)
        ops[eng].append(run)

    def dma(eng, sem, out, in_, wait=(), transpose=False):
        if transpose:
            op(eng, lambda e: e.dma_start_transpose(out, in_), wait=wait,
               inc=[(sem, 16)])
        else:
            op(eng, lambda e: e.dma_start(out=out, in_=in_), wait=wait,
               inc=[(sem, 16)])

    dma("y", sC, bqkvS[:, :], bqkv_d[:, :])
    dma("y", sC, bvS[:, :], bv_d.broadcast_to([128, D]))
    dma("y", sC, boS[:, :], bo_d.broadcast_to([128, D]))
    const_c = sC.n
    op("v", lambda e: e.memset(ones66[:, :], 1.0))
    op("v", lambda e: e.memset(amx3[:, :], 1.0))
    op("v", lambda e: e.memset(stat[:, :], 1.0))
    op("v", lambda e: e.memset(dmy[:, :], 0.0), inc=[(sV, 1)])

    qrec = []
    qctr = [0]
    slot_guard = {0: [], 1: [], 2: [], 3: []}

    def q_emit_ts(r):
        a, sc, _ = r["sv"]
        op("v", lambda e, sc=sc, a=a: e.tensor_scalar(
            sc[:, :], a[:, :], 1.0 / 6.0, 1e-12, Alu.mult, Alu.max))

    def q_emit_recip(r):
        _, sc, rr = r["sv"]
        op("v", lambda e, rr=rr, sc=sc: e.reciprocal(rr[:, :], sc[:, :]))

    def q_emit_apply(r):
        w, nb = r["w"], r["w"] // 16
        _, sc, rr = r["sv"]
        src3 = r["src"].rearrange("p (b k) -> p b k", k=16)
        rc3 = rr[:, 0:nb].unsqueeze(2).broadcast_to([128, nb, 16])
        sc3 = sc[:, 0:nb].unsqueeze(2).broadcast_to([128, nb, 16])
        nv = n_[:, 0:w]
        n3 = nv.rearrange("p (b k) -> p b k", k=16)
        op("v", lambda e, n3=n3, src3=src3, rc3=rc3: e.tensor_tensor(
            n3, src3, rc3, Alu.mult),
            wait=list(slot_guard[r["slot"]]), inc=[(sV, 1)])
        nready = sV.n
        op("s", lambda e, w=w, nv=nv: e.activation(sg_[:, 0:w], nv, Act.Sign),
           wait=[(sV, nready)], inc=[(sA, 1)])
        sgnready = sA.n
        av = a_[:, 0:w]
        op("s", lambda e, av=av, nv=nv: e.activation(av, nv, Act.Abs),
           inc=[(sA, 1)])
        absready = sA.n
        op("v", lambda e, av=av, w=w: e.tensor_scalar(
            acA[:, 0:w], av, 0.25, None, Alu.is_gt), wait=[(sA, absready)])
        for t in (0.75, 1.25, 1.75):
            op("v", lambda e, t=t, av=av, w=w: e.scalar_tensor_tensor(
                acA[:, 0:w], av, t, acA[:, 0:w], Alu.is_gt, Alu.add))
        op("v", lambda e, av=av, w=w: e.tensor_scalar(
            acB[:, 0:w], av, 5.0, 2.0, Alu.is_gt, Alu.mult))
        for t in (2.5, 3.5):
            op("v", lambda e, t=t, av=av, w=w: e.scalar_tensor_tensor(
                acB[:, 0:w], av, t, acB[:, 0:w], Alu.is_gt, Alu.add))
        op("v", lambda e, w=w: e.scalar_tensor_tensor(
            acA[:, 0:w], acA[:, 0:w], 0.5, acB[:, 0:w], Alu.mult, Alu.add))
        s3 = ss_[:, 0:w].rearrange("p (b k) -> p b k", k=16)
        sg3 = sg_[:, 0:w].rearrange("p (b k) -> p b k", k=16)
        op("v", lambda e, s3=s3, sg3=sg3, sc3=sc3: e.tensor_tensor(
            s3, sg3, sc3, Alu.mult), wait=[(sA, sgnready)])
        op("v", lambda e, dst=r["dst"], w=w: e.tensor_tensor(
            dst[:, 0:w], acA[:, 0:w], ss_[:, 0:w], Alu.mult), inc=[(sV, 1)])
        vready = sV.n
        flush_tr([(sV, vready)])
        tr = r["tr"]
        for j, (sem, dsl, extra) in enumerate(tr):
            pend_tr.append((sem, dsl, r["dst"][:, j * 128:(j + 1) * 128],
                            extra))
        if tr:
            slot_guard[r["slot"]] = [(tr[0][0], tr[0][0].n + len(tr) * 16)]
        if r["post"] is not None:
            r["post"](vready)

    def q_push(src, w, tr=(), wait_red=(), post=None):
        slot = qctr[0] % 4
        qctr[0] += 1
        sv = qslot_views(slot)
        nb = w // 16
        src3 = src.rearrange("p (b k) -> p b k", k=16)
        a = sv[0]
        op("v", lambda e, a=a, src3=src3, nb=nb: e.tensor_reduce(
            a[:, 0:nb], src3, AX, Alu.max, apply_absolute_value=True),
            wait=wait_red)
        qrec.append({"w": w, "sv": sv, "slot": slot, "src": src,
                     "dst": qtile[:, slot * 512: slot * 512 + 512],
                     "tr": list(tr), "post": post})
        if len(qrec) >= 2:
            q_emit_ts(qrec[-2])
        if len(qrec) >= 3:
            q_emit_recip(qrec[-3])
        if len(qrec) >= 4:
            q_emit_apply(qrec.pop(0))

    def q_flush():
        assert len(qrec) == 3, f"q_flush with {len(qrec)} pending"
        for _ in range(3):
            q_push(dmy[:, 0:256], 256)
        c0, c1, c2 = qrec
        q_emit_apply(c0)
        q_emit_ts(c2)
        q_emit_recip(c1)
        q_emit_apply(c1)
        q_emit_recip(c2)
        q_emit_apply(c2)
        qrec.clear()

    def ln_chain(srcb, dst, wait_first, wait_norm):
        src8 = srcb.rearrange("p (c w) -> p c w", w=128)
        op("s", lambda e: e.activation(xsqv, srcb, Act.Square),
           wait=list(wait_first) + [(sV, sV.n)], inc=[(sA, 1)])
        sqd = sA.n
        op("v", lambda e: e.tensor_reduce(st_sx[:, 0:1], srcb, AX, Alu.add),
           wait=wait_first)
        op("v", lambda e: e.tensor_reduce(st_sq[:, 0:1], xsqv, AX, Alu.add),
           wait=[(sA, sqd)])
        for k in (1, 2, 4):
            op("v", lambda e, k=k: e.tensor_copy(st_sx[:, k:2 * k], st_sx[:, 0:k]))
            op("v", lambda e, k=k: e.tensor_copy(st_sq[:, k:2 * k], st_sq[:, 0:k]))
        op("v", lambda e: e.tensor_scalar(st_mean, st_sx, 1.0 / D, None, Alu.mult))
        op("v", lambda e: e.tensor_scalar(st_ex2, st_sq, 1.0 / D, None, Alu.mult))
        op("v", lambda e: e.tensor_tensor(st_m2, st_mean, st_mean, Alu.mult))
        op("v", lambda e: e.tensor_tensor(st_var, st_ex2, st_m2, Alu.subtract))
        op("v", lambda e: e.tensor_scalar(st_va, st_var, EPS, None, Alu.add),
           inc=[(sV, 1)])
        vv = sV.n
        op("s", lambda e: e.activation(st_rs, st_va, Act.Ln), wait=[(sV, vv)])
        op("s", lambda e: e.activation(st_rs, st_rs, Act.Exp, scale=-0.5),
           inc=[(sA, 1)])
        aa = sA.n
        op("v", lambda e: e.engine_nop(), wait=[(sA, aa)] + wait_norm)
        mb = st_mean[:, 0:8].unsqueeze(2).broadcast_to([128, 8, 128])
        rb = st_rs[:, 0:8].unsqueeze(2).broadcast_to([128, 8, 128])
        xc8 = xsqv.rearrange("p (c w) -> p c w", w=128)
        op("v", lambda e: e.tensor_tensor(xc8, src8, mb, Alu.subtract))
        op("v", lambda e: e.tensor_tensor(
            dst.rearrange("p (c w) -> p c w", w=128), xc8, rb, Alu.mult),
            inc=[(sV, 1)])
        return sV.n

    pend_tr = []

    def flush_tr(wait_v):
        for i, (sem, dst, srcap, extra) in enumerate(pend_tr):
            dma("y", sem, dst, srcap,
                wait=(list(wait_v) + extra) if i == 0 else [], transpose=True)
        pend_tr.clear()

    def drain_dummy():
        op("v", lambda e: e.tensor_copy(dmy[:, 0:512], dmy[:, 512:1024]),
           inc=[(sV, 1)])
        return sV.n

    wq_buf_free = [0, 0, 0, 0]
    wfs = [pool[:, 0:128], pool[:, 128:256], pool[:, 256:384],
           pool[:, 384:512]]
    dma("y", sWF[0], pool[:, 3072:5888], wds_d[:, :])
    wd_dv = sWF[0].n
    wti = 0
    for mi, (src_d, strip0) in enumerate([(wgs_d, 0), (wus_d, 2816)]):
        for it in range(ICN):
            sl = wti % 4
            wti += 1
            dma("y", sWF[sl], wfs[sl][:, :], src_d[it * 128:(it + 1) * 128, :],
                wait=[(sV, wq_buf_free[sl])])
            dv = sWF[sl].n
            tr = [(sTQ[qctr[0] % 2],
                   slabA[:, strip0 + it * 128: strip0 + it * 128 + 128], [])]

            def post(vready, sl=sl):
                wq_buf_free[sl] = sV.n
            q_push(wfs[sl][:, :], 128, tr=tr, wait_red=[(sWF[sl], dv)],
                   post=post)
    for ck in range(6):
        c0 = ck * 512
        w = min(512, II - c0)
        tr = []
        for j in range(w // 128):
            c = (c0 + j * 128) // 128
            tr.append((sTQ[qctr[0] % 2],
                       slabA[:, 5632 + c * 128: 5632 + c * 128 + 128], []))
        q_push(pool[:, 3072 + c0:3072 + c0 + w], w, tr=tr,
               wait_red=[(sWF[0], wd_dv)])
    q_flush()
    tqw = [(sTQ[0], sTQ[0].n), (sTQ[1], sTQ[1].n)]
    dma("y", sWST, wq_st[0], slabA[:, 0:2816], wait=tqw)
    dma("y", sWST, wq_st[1], slabA[:, 2816:5632])
    dma("y", sWST, wq_st[2], slabA[:, 5632:8448])
    w_done = sWST.n
    w_quant_v = sV.n

    def emit_collective():
        op("g", lambda e: e.collective_compute(
            "AllGather", Alu.bypass, [list(range(8))],
            ins=[wq_st[:, :, :]],
            outs=[G_all[:, :, :, :]]),
            wait=[(sWST, w_done)], inc=[(sCC, 1)])
        cc = sCC.n
        if os.environ.get("KTAP") == "1":
            dma("y", sC, wgT_q.rearrange("(r p) w -> r p w", r=8),
                G_all[:, 0, :, :], wait=[(sCC, cc)])
            dma("y", sC, wuT_q.rearrange("(r p) w -> r p w", r=8),
                G_all[:, 1, :, :])
            for rr in range(8):
                dma("y", sC,
                    wdT_q[:, rr * 128:(rr + 1) * 128]
                    .rearrange("(c q) d -> c q d", q=128),
                    G_all[rr, 2, :, :].rearrange("q (c d) -> c q d", d=128))
        return cc

    xb_free = [0, 0]
    x1b_free = [0, 0]
    ob_free = [0, 0]
    qk_free = [0, 0]
    hb_free = [0, 0]
    wgu_free = 0
    wdb_free = [0, 0]
    qp_free = [0, 0]
    kp_free = [0, 0]
    vp_free = [0, 0]
    PT_free = [0, 0]
    vst_free = [0, 0]
    xrst = [None] * TT
    x1st = [None] * TT

    def load_wqkv(wait0):
        for dc in range(DC):
            dma("g", sWKV, slabA[:, dc * 3072:(dc + 1) * 3072],
                wqkvT_d[dc * 128:(dc + 1) * 128, :],
                wait=(wait0 if dc == 0 else []))
        return sWKV.n

    wkv_ready = load_wqkv([(sWST, w_done)])
    cc_done = emit_collective()
    exp_done = 0
    down_evac_v = 0

    for s in range(NSEQ):
        seq_start_p = sP.n

        for tt in range(TT):
            sl = tt % 2
            dma("g", sX[sl], xb[sl][:, :], x_d[s, tt * 128:(tt + 1) * 128, :],
                wait=[(sV, xb_free[sl])] +
                     ([(sV, w_quant_v), (sWST, w_done)]
                      if (s == 0 and tt == 0) else []))
            dx = sX[sl].n
            qsl = tt % 2
            qt_ = qtile[:, qsl * 1024:(qsl + 1) * 1024]
            vq = ln_chain(xb[sl][:, :], qt_,
                          wait_first=[(sX[sl], dx)],
                          wait_norm=slot_guard[2 * qsl] +
                                    slot_guard[2 * qsl + 1])
            op("v", lambda e, o=ob[sl], b=xb[sl], q=qt_: e.tensor_tensor(
                o, b[:, :], q, Alu.add),
                wait=[(sOB[sl], ob_free[sl])], inc=[(sV, 1)])
            xb_free[sl] = sV.n
            dma("y", sOB[sl], xr_d[tt], ob[sl], wait=[(sV, sV.n)])
            ob_free[sl] = sOB[sl].n
            xrst[tt] = (sl, sOB[sl].n)
            flush_tr([(sV, vq)])
            for c in range(DC):
                extra = [(sP, seq_start_p)] if (tt == 0 and c == 0) else []
                pend_tr.append((sTQ[qsl],
                                slabB[:, c * 2048 + tt * 128:
                                      c * 2048 + tt * 128 + 128],
                                qt_[:, c * 128:(c + 1) * 128], extra))
            slot_guard[2 * qsl] = [(sTQ[qsl], sTQ[qsl].n + 8 * 16)]
            slot_guard[2 * qsl + 1] = [(sTQ[qsl], sTQ[qsl].n + 8 * 16)]
        flush_tr([(sV, drain_dummy())])
        xnT0, xnT1 = sTQ[0].n, sTQ[1].n

        evac_prev = [0, 0, 0, 0]
        for ec in range(2 * DC):
            ssl = ec % 2
            stg = qkstage[:, ssl * 2048:(ssl + 1) * 2048]
            sQKs = [sQK, sQK2][ssl]
            for qt in range(4):
                for dc in range(DC):
                    w0 = []
                    if dc == 0:
                        w0 = [(sA, evac_prev[qt])]
                        if qt == 0:
                            w0 += [(sV, sV.n)]
                        if ec == 0 and qt == 0:
                            w0 += [(sWKV, wkv_ready), (sTQ[0], xnT0),
                                   (sTQ[1], xnT1)]
                    op("t", lambda e, ec=ec, dc=dc, qt=qt: e.matmul(
                        psA[:, qt * 512:(qt + 1) * 512],
                        slabA[:, dc * 3072 + ec * 128: dc * 3072 + ec * 128 + 128],
                        slabB[:, dc * 2048 + qt * 512: dc * 2048 + qt * 512 + 512],
                        start=(dc == 0), stop=(dc == DC - 1)),
                        wait=w0,
                        inc=([(sP, 1)] if dc == DC - 1 else []))
                mmv = sP.n
                op("s", lambda e, ec=ec, qt=qt, stg=stg: e.activation(
                    stg[:, qt * 512:(qt + 1) * 512],
                    psA[:, qt * 512:(qt + 1) * 512],
                    Act.Identity, bias=bqkvS[:, ec:ec + 1]),
                    wait=[(sP, mmv)] +
                         ([(sQKs, qk_free[ssl])] if qt == 0 else []) +
                         ([(sC, const_c)] if (s == 0 and ec == 0 and qt == 0)
                          else []),
                    inc=[(sA, 1)])
                evac_prev[qt] = sA.n
            ev = sA.n
            dst = (qS_d if ec < DC else kS_d)[ec % DC]
            dma("y", sQKs, dst[:, :], stg, wait=[(sA, ev)])
            qk_free[ssl] = sQKs.n

        for tt in range(TT):
            pse = [ps2a, ps2b][tt % 2]
            for dc in range(DC):
                for nh2 in range(2):
                    w0 = []
                    if dc == 0 and nh2 == 0:
                        w0 = [(sV, sV.n)]
                    op("t", lambda e, tt=tt, dc=dc, nh2=nh2, p=pse: e.matmul(
                        p[:, nh2 * 512:(nh2 + 1) * 512],
                        slabB[:, dc * 2048 + tt * 128: dc * 2048 + tt * 128 + 128],
                        slabA[:, dc * 3072 + 2048 + nh2 * 512:
                              dc * 3072 + 2048 + nh2 * 512 + 512],
                        start=(dc == 0), stop=(dc == DC - 1)),
                        wait=w0,
                        inc=([(sP, 1)] if (dc == DC - 1 and nh2 == 1) else []))
            mmv = sP.n
            vsl = tt % 2
            vstage = PTb[:, vsl * 2048:vsl * 2048 + 1040]
            v3 = vstage.rearrange("p (h c) -> p h c", c=65)
            op("v", lambda e, v3=v3: e.memset(v3[:, :, 64:65], 1.0),
               wait=[(sP, mmv), (sVS[vsl], vst_free[vsl])] +
                    ([(sC, const_c)] if (s == 0 and tt == 0) else []))
            op("v", lambda e, v3=v3, p=pse: e.tensor_tensor(
                v3[:, :, 0:64], p[:, :].rearrange("p (h c) -> p h c", c=64),
                bvS[:, :].rearrange("p (h c) -> p h c", c=64), Alu.add),
                inc=[(sV, 1)])
            dma("y", sVS[vsl], vS_d[tt], vstage, wait=[(sV, sV.n)])
            vst_free[vsl] = sVS[vsl].n
        stage_done = [(sQK, sQK.n), (sQK2, sQK2.n),
                      (sVS[0], sVS[0].n), (sVS[1], sVS[1].n)]
        qkv_done_p = sP.n
        qkv_done_v = sV.n

        av_guard_v = qkv_done_v
        rep_guard_v = qkv_done_v
        rep_guard_p = sP.n
        if exp_done == 0:
            exp_done = sA.n
        hb_slot = 0
        first_exp = True
        for p in range(DC):
            slq = p % 2
            dma("y", sPQ[slq], qp[:, slq * 2048:(slq + 1) * 2048], qS_d[p],
                wait=[(sP, qp_free[slq])] + (stage_done if p < 2 else []))
            dma("y", sPK[slq], kp[:, slq * 2048:(slq + 1) * 2048], kS_d[p],
                wait=[(sP, kp_free[slq])])
            for g4 in range(4):
                dma("y", sPV[slq],
                    vp[:, slq * 2080 + g4 * 520: slq * 2080 + (g4 + 1) * 520]
                    .rearrange("p (t c) -> p t c", c=130),
                    vS_d[g4 * 4:(g4 + 1) * 4, :, p * 130:(p + 1) * 130]
                    .transpose([1, 0, 2]),
                    wait=[(sP, vp_free[slq])] if g4 == 0 else [])
            pair_ld = [(sPQ[slq], sPQ[slq].n), (sPK[slq], sPK[slq].n),
                       (sPV[slq], sPV[slq].n)]
            qpb = qp[:, slq * 2048:(slq + 1) * 2048]
            kpb = kp[:, slq * 2048:(slq + 1) * 2048]
            vpb = vp[:, slq * 2080:(slq + 1) * 2080].rearrange(
                "p (t c) -> p t c", c=130)
            for qc in range(4):
                for ktp in range(8):
                    ptsl = ktp % 2
                    PT = PTb[:, ptsl * 2048:(ptsl + 1) * 2048]
                    for kt2 in range(2):
                        kt = ktp * 2 + kt2
                        w0 = []
                        if kt2 == 0:
                            w0 = [(sA, exp_done)]
                            if qc == 0 and ktp == 0:
                                w0 += pair_ld
                        op("t", lambda e, kt=kt, kt2=kt2, q=qpb, k=kpb, qc=qc:
                            e.matmul(
                                psA[:, kt2 * 512:(kt2 + 1) * 512],
                                k[0:64, kt * 128:(kt + 1) * 128],
                                q[0:64, qc * 512:(qc + 1) * 512],
                                start=True, stop=True, tile_position=(0, 0)),
                            wait=w0)
                        op("t", lambda e, kt=kt, kt2=kt2, q=qpb, k=kpb, qc=qc:
                            e.matmul(
                                psA[:, 1024 + kt2 * 512: 1024 + (kt2 + 1) * 512],
                                k[64:128, kt * 128:(kt + 1) * 128],
                                q[64:128, qc * 512:(qc + 1) * 512],
                                start=True, stop=True, tile_position=(64, 0)),
                            inc=([(sP, 1)] if kt2 == 1 else []))
                    sc_done = sP.n
                    ew = [(sP, sc_done), (sP, PT_free[ptsl])]
                    if first_exp:
                        ew += stage_done
                        if ktp == 1:
                            first_exp = False
                    op("s", lambda e, PT=PT: e.activation(PT, psA[:, :], Act.Exp,
                                                          scale=SCALE),
                       wait=ew, inc=[(sA, 1)])
                    exp_done = sA.n
                    for kt2 in range(2):
                        kt = ktp * 2 + kt2
                        w0 = []
                        if kt2 == 0:
                            w0 = [(sA, exp_done)]
                            if ktp == 0:
                                w0 += [(sV, av_guard_v)]
                        op("t", lambda e, kt=kt, kt2=kt2, ktp=ktp, v=vpb, PT=PT:
                            e.matmul(
                                ps2a[0:65, 0:512], v[:, kt, 0:65],
                                PT[:, kt2 * 512:(kt2 + 1) * 512],
                                start=(ktp == 0 and kt2 == 0),
                                stop=(ktp == 7 and kt2 == 1)),
                            wait=w0)
                        op("t", lambda e, kt=kt, kt2=kt2, ktp=ktp, v=vpb, PT=PT:
                            e.matmul(
                                ps2a[0:65, 512:1024], v[:, kt, 65:130],
                                PT[:, 1024 + kt2 * 512: 1024 + (kt2 + 1) * 512],
                                start=(ktp == 0 and kt2 == 0),
                                stop=(ktp == 7 and kt2 == 1)),
                            inc=([(sP, 1)] if kt2 == 1 else []))
                    PT_free[ptsl] = sP.n
                av_done = sP.n
                op("s", lambda e: e.activation(strip[64:65, 0:1024],
                                               ps2a[64:65, :], Act.Ln),
                   wait=[(sP, av_done), (sP, rep_guard_p)])
                op("s", lambda e: e.activation(strip[64:65, 0:1024],
                                               strip[64:65, 0:1024], Act.Exp,
                                               scale=-1.0), inc=[(sA, 1)])
                rec_done = sA.n
                for nh2 in range(2):
                    op("t", lambda e, nh2=nh2: e.matmul(
                        ps2b[0:65, nh2 * 512:(nh2 + 1) * 512],
                        ones66[64:65, 0:65],
                        strip[64:65, nh2 * 512:(nh2 + 1) * 512],
                        start=True, stop=True, tile_position=(64, 0)),
                        wait=([(sV, rep_guard_v), (sA, rec_done)]
                              if nh2 == 0 else []),
                        inc=([(sP, 1)] if nh2 == 1 else []))
                rep_guard_p = sP.n
                op("v", lambda e: e.tensor_copy(rbr[0:64, :], ps2b[0:64, :]),
                   wait=[(sP, rep_guard_p)], inc=[(sV, 1)])
                rep_guard_v = sV.n
                op("v", lambda e, p=p, qc=qc: e.tensor_tensor(
                    slabB[0:64, p * 2048 + qc * 512: p * 2048 + qc * 512 + 512],
                    ps2a[0:64, 0:512], rbr[0:64, 0:512], Alu.mult))
                hsl = hb_slot % 2
                op("v", lambda e, hsl=hsl: e.tensor_tensor(
                    hbnc[0:64, hsl * 512:(hsl + 1) * 512],
                    ps2a[0:64, 512:1024], rbr[0:64, 512:1024], Alu.mult),
                    wait=[(sHBL[hsl], hb_free[hsl])], inc=[(sV, 1)])
                av_guard_v = sV.n
                dma("y", sHB[hsl], hb_d[hsl], hbnc[0:64, hsl * 512:(hsl + 1) * 512],
                    wait=[(sV, av_guard_v)])
                wb = sHB[hsl].n
                dma("g", sHBL[hsl],
                    slabB[64:128, p * 2048 + qc * 512: p * 2048 + qc * 512 + 512],
                    hb_d[hsl], wait=[(sHB[hsl], wb)])
                hb_free[hsl] = sHBL[hsl].n
                hb_slot += 1
            qp_free[slq] = sP.n
            kp_free[slq] = sP.n
            vp_free[slq] = sP.n
        attn_done_hbl = [(sHBL[0], sHBL[0].n), (sHBL[1], sHBL[1].n)]
        attn_done_v = sV.n

        dma("g", sWKV, woTS.rearrange("p (c d2) -> p c d2", d2=1024),
            woT_d.rearrange("(c p) d2 -> p c d2", p=128),
            wait=[(sP, qkv_done_p)])
        woTS_g = sWKV.n
        oproj_evac_v = 0
        for tt in range(TT):
            pse = [ps2a, ps2b][tt % 2]
            for dc in range(DC):
                for nh2 in range(2):
                    w0 = []
                    if dc == 0 and nh2 == 0:
                        w0 = [(sV, attn_done_v if tt < 2 else oproj_evac_v)]
                        if tt == 0:
                            w0 += attn_done_hbl + [(sWKV, woTS_g),
                                                   (sV, attn_done_v)]
                    op("t", lambda e, tt=tt, dc=dc, nh2=nh2, p=pse: e.matmul(
                        p[:, nh2 * 512:(nh2 + 1) * 512],
                        slabB[:, dc * 2048 + tt * 128: dc * 2048 + tt * 128 + 128],
                        woTS[:, dc * 1024 + nh2 * 512: dc * 1024 + nh2 * 512 + 512],
                        start=(dc == 0), stop=(dc == DC - 1)),
                        wait=w0,
                        inc=([(sP, 1)] if (dc == DC - 1 and nh2 == 1) else []))
            mmv = sP.n
            sl = tt % 2
            dma("g", sX[sl], xb[sl][:, :], xr_d[tt],
                wait=[(sV, xb_free[sl]), (sOB[xrst[tt][0]], xrst[tt][1])])
            dx = sX[sl].n
            op("v", lambda e, o=ob[sl], p2=pse: e.tensor_tensor(
                o, p2[:, :], boS[:, :], Alu.add),
                wait=[(sP, mmv), (sX[sl], dx), (sOB[sl], ob_free[sl])] +
                     ([(sC, const_c)] if (s == 0 and tt == 0) else []))
            op("v", lambda e, o=ob[sl], b=xb[sl]: e.tensor_tensor(
                o, o, b[:, :], Alu.add), inc=[(sV, 1)])
            xb_free[sl] = sV.n
            oproj_evac_v = sV.n
            dma("y", sOB[sl], x1_d[tt], ob[sl], wait=[(sV, oproj_evac_v)])
            ob_free[sl] = sOB[sl].n
            x1st[tt] = (sl, sOB[sl].n)
        oproj_done_p = sP.n
        oproj_done_v = sV.n

        ln_slots = [xt, ob[0], ob[1]]
        for tt in range(TT):
            sl = tt % 2
            dma("g", sX1[sl], x1b[sl][:, :], x1_d[tt],
                wait=[(sV, x1b_free[sl]), (sOB[x1st[tt][0]], x1st[tt][1])])
            dx = sX1[sl].n
            lsl = tt % 3
            wn = [(sOB[lsl - 1], ob_free[lsl - 1])] if (tt in (1, 2)) else []
            ln_chain(x1b[sl][:, :], ln_slots[lsl][:, :],
                     wait_first=[(sX1[sl], dx)], wait_norm=wn)
            x1b_free[sl] = sV.n
            for ck in range(2):
                tr = []
                for j in range(4):
                    c = ck * 4 + j
                    extra = ([(sP, oproj_done_p)]
                             if (tt == 0 and ck == 0 and j == 0) else [])
                    tr.append((sTQ[qctr[0] % 2],
                               slabB[:, c * 2048 + tt * 128:
                                     c * 2048 + tt * 128 + 128], extra))
                q_push(ln_slots[lsl][:, ck * 512:(ck + 1) * 512], 512, tr=tr)
        q_flush()
        xqT = [(sTQ[0], sTQ[0].n), (sTQ[1], sTQ[1].n)]

        gu_evac_v = oproj_done_v
        silu_done = sA.n
        for tg in range(2):
            tg_start_p = sP.n
            for icb in range(ICB):
                dma("g", sWGU, wgu[:, 0:2048].rearrange("p (c w) -> p c w", w=256),
                    G_all[:, 0, :, icb * 256:(icb + 1) * 256]
                    .rearrange("c p w -> p c w"),
                    wait=[(sP, wgu_free)] +
                         ([(sCC, cc_done)] + xqT if (tg == 0 and icb < 1)
                          else []))
                dma("g", sWGU, wgu[:, 2048:4096].rearrange("p (c w) -> p c w", w=256),
                    G_all[:, 1, :, icb * 256:(icb + 1) * 256]
                    .rearrange("c p w -> p c w"))
                wld = sWGU.n
                for tt8 in range(8):
                    tt = tg * 8 + tt8
                    gb = (tt8 % 2) * 1024
                    for dc in range(DC):
                        w0 = []
                        if dc == 0:
                            w0 = [(sWGU, wld), (sV, gu_evac_v), (sA, silu_done),
                                  (sV, down_evac_v)]
                            if tg == 0 and icb == 0 and tt8 == 0:
                                w0 += [(sP, qkv_done_p), (sA, exp_done)]
                        op("t", lambda e, tt=tt, dc=dc, gb=gb: e.matmul(
                            psA[:, gb:gb + 256],
                            slabB[:, dc * 2048 + tt * 128:
                                  dc * 2048 + tt * 128 + 128],
                            wgu[:, dc * 256:(dc + 1) * 256],
                            start=(dc == 0), stop=(dc == DC - 1)), wait=w0)
                        op("t", lambda e, tt=tt, dc=dc, gb=gb: e.matmul(
                            psA[:, gb + 512:gb + 768],
                            slabB[:, dc * 2048 + tt * 128:
                                  dc * 2048 + tt * 128 + 128],
                            wgu[:, 2048 + dc * 256:2048 + (dc + 1) * 256],
                            start=(dc == 0), stop=(dc == DC - 1)),
                            inc=([(sP, 1)] if dc == DC - 1 else []))
                    guv = sP.n
                    op("s", lambda e, gb=gb: e.activation(
                        sgb[:, 0:256], psA[:, gb:gb + 256], Act.Silu),
                        wait=[(sP, guv)], inc=[(sA, 1)])
                    silu_done = sA.n
                    hsl = qctr[0] % 4
                    hh_ = strip[:, hsl * 256:(hsl + 1) * 256]
                    op("v", lambda e, gb=gb, hh_=hh_: e.tensor_tensor(
                        hh_, sgb[:, 0:256], psA[:, gb + 512:gb + 768],
                        Alu.mult), wait=[(sA, silu_done)], inc=[(sV, 1)])
                    gu_evac_v = sV.n
                    tr = []
                    for j in range(2):
                        c = icb * 2 + j
                        extra = ([(sP, tg_start_p)]
                                 if (icb == 0 and tt8 == 0 and j == 0) else [])
                        tr.append((sTH[qctr[0] % 2],
                                   slabA[:, c * 1024 + tt8 * 128:
                                         c * 1024 + tt8 * 128 + 128], extra))
                    q_push(hh_, 256, tr=tr)
                wgu_free = sP.n
            q_flush()
            hqT = [(sTH[0], sTH[0].n), (sTH[1], sTH[1].n)]
            for sub in range(2):
                psv = [psA[:, 0:1024], psA[:, 1024:2048], ps2a[:, :], ps2b[:, :]]
                for ic in range(ICN):
                    wsl = ic % 2
                    dma("g", sWD[wsl], wdb[:, wsl * 1024:(wsl + 1) * 1024]
                        .rearrange("q (r d) -> q r d", r=8),
                        G_all[:, 2, :, ic * 128:(ic + 1) * 128]
                        .rearrange("r q d -> q r d"),
                        wait=[(sP, wdb_free[wsl])] +
                             (hqT + [(sCC, cc_done)] if ic < 2 else []))
                    wld = sWD[wsl].n
                    for t4 in range(4):
                        tt8 = sub * 4 + t4
                        for nh2 in range(2):
                            w0 = []
                            if t4 == 0 and nh2 == 0:
                                w0 = [(sWD[wsl], wld)]
                                if ic == 0:
                                    w0 += [(sV, gu_evac_v), (sA, silu_done),
                                           (sV, down_evac_v if down_evac_v else
                                            oproj_done_v), (sV, attn_done_v)]
                            op("t", lambda e, ic=ic, tt8=tt8, t4=t4, nh2=nh2,
                                wsl=wsl: e.matmul(
                                    psv[t4][:, nh2 * 512:(nh2 + 1) * 512],
                                    slabA[:, ic * 1024 + tt8 * 128:
                                          ic * 1024 + tt8 * 128 + 128],
                                    wdb[:, wsl * 1024 + nh2 * 512:
                                        wsl * 1024 + nh2 * 512 + 512],
                                    start=(ic == 0), stop=(ic == ICN - 1)),
                                wait=w0,
                                inc=([(sP, 1)] if (t4 == 3 and nh2 == 1)
                                     else []))
                    wdb_free[wsl] = sP.n
                dn = sP.n
                for t4 in range(4):
                    tt = tg * 8 + sub * 4 + t4
                    sl = t4 % 2
                    dma("g", sX1[sl], x1b[sl][:, :], x1_d[tt],
                        wait=[(sV, x1b_free[sl]),
                              (sOB[x1st[tt][0]], x1st[tt][1])])
                    dx = sX1[sl].n
                    op("v", lambda e, o=ob[sl], pv=psv[t4], b=x1b[sl]:
                        e.tensor_tensor(o, pv, b[:, :], Alu.add),
                        wait=[(sP, dn), (sX1[sl], dx), (sOB[sl], ob_free[sl])],
                        inc=[(sV, 1)])
                    x1b_free[sl] = sV.n
                    down_evac_v = sV.n
                    dma("y", sOB[sl], out_d[s, tt * 128:(tt + 1) * 128, :],
                        ob[sl], wait=[(sV, down_evac_v)])
                    ob_free[sl] = sOB[sl].n
        if s == 0:
            wkv_ready = load_wqkv([(sP, sP.n)])

    fin0, fin1, fin_c = sOB[0].n, sOB[1].n, sC.n

    with nc.Block() as block:
        @block.tensor
        def _(t):
            for fn in ops["t"]:
                fn(t)

        @block.vector
        def _(v):
            for fn in ops["v"]:
                fn(v)

        @block.scalar
        def _(s_):
            for fn in ops["s"]:
                fn(s_)

        @block.gpsimd
        def _(g):
            for fn in ops["g"]:
                fn(g)

        @block.sync
        def _(y):
            for fn in ops["y"]:
                fn(y)
            y.wait_ge(sOB[0].h, fin0)
            y.wait_ge(sOB[1].h, fin1)
            y.wait_ge(sC.h, fin_c)

    es.close()
    return nc


def kernel(L0_attn_x, L0_attn_w_qkv, L0_attn_b_qkv, L0_attn_w_o, L0_attn_b_o,
           L0_ffn_w_gate, L0_ffn_w_up, L0_ffn_w_down):
    x = np.ascontiguousarray(np.asarray(L0_attn_x, dtype=np.float32))
    w_qkv = np.asarray(L0_attn_w_qkv, dtype=np.float32)
    b_qkv = np.asarray(L0_attn_b_qkv, dtype=np.float32)
    w_o = np.asarray(L0_attn_w_o, dtype=np.float32)
    b_o = np.asarray(L0_attn_b_o, dtype=np.float32)
    wg = np.ascontiguousarray(np.asarray(L0_ffn_w_gate, dtype=np.float32))
    wu = np.ascontiguousarray(np.asarray(L0_ffn_w_up, dtype=np.float32))
    wd = np.ascontiguousarray(np.asarray(L0_ffn_w_down, dtype=np.float32))

    wqkvT = np.ascontiguousarray(w_qkv.T)
    woT = np.ascontiguousarray(w_o.T)
    bqkv_qk = np.ascontiguousarray(b_qkv[:2048].reshape(16, 128).T)
    bv = np.ascontiguousarray(b_qkv[2048:].reshape(1, 1024))
    bo = np.ascontiguousarray(b_o.reshape(1, 1024))

    nc = build_nc()
    in_maps = []
    for i in range(8):
        in_maps.append({
            "x": np.ascontiguousarray(x[2 * i:2 * i + 2]),
            "wqkvT": wqkvT, "woT": woT, "bqkv": bqkv_qk, "bv": bv, "bo": bo,
            "wgs": np.ascontiguousarray(wg[:, i * 128:(i + 1) * 128]),
            "wus": np.ascontiguousarray(wu[:, i * 128:(i + 1) * 128]),
            "wds": np.ascontiguousarray(wd[i * 128:(i + 1) * 128, :]),
        })
    res = run_bass_kernel_spmd(nc, in_maps, core_ids=list(range(8)))
    return np.concatenate([np.asarray(r["out"]) for r in res.results], axis=0)
